# version 04
# speedup vs baseline: 1.1256x; 1.0135x over previous
import numpy as np
from contextlib import ExitStack

import concourse.bass as bass
import concourse.mybir as mybir
from concourse.bass_utils import run_bass_kernel_spmd

F32 = mybir.dt.float32
BF16 = mybir.dt.bfloat16
I32 = mybir.dt.int32
AF = mybir.ActivationFunctionType
ALU = mybir.AluOpType
AX = mybir.AxisListType

D = 2048
S = 4096
TOK = 2048
NDCH = D // 128
IFFN = 8192
NICH = IFFN // 128


class Sem:
    def __init__(self, h):
        self.h = h
        self.n = 0

    def inc(self, k=1):
        self.n += k
        return self.n


class Plan:
    ENGINES = ("sync", "tensor", "vector", "scalar", "gpsimd")

    def __init__(self, nc):
        self.nc = nc
        self.ops = {e: [] for e in self.ENGINES}

    def on(self, eng):
        def deco(fn):
            self.ops[eng].append(fn)
            return None
        return deco

    def dma(self, queue, out, in_, sem, waits=(), **kw):
        v = sem.inc(16)
        def fn(e, out=out, in_=in_, kw=kw, waits=tuple(waits)):
            for (ws, wv) in waits:
                if wv > 0:
                    e.wait_ge(ws.h, wv)
            e.dma_start(out=out, in_=in_, **kw).then_inc(sem.h, 16)
        self.ops[queue].append(fn)
        return v

    def emit(self):
        with self.nc.Block() as block:
            for name in self.ENGINES:
                ops = self.ops[name]
                if not ops:
                    continue
                def section(eng, _ops=ops):
                    for fn in _ops:
                        fn(eng)
                getattr(block, name)(section)
        self.ops = {e: [] for e in self.ENGINES}


def mk_sems(es, nc, names):
    return {n: Sem(es.enter_context(nc.semaphore(n))) for n in names}



def build_B():
    nc = bass.Bass("TRN2", target_bir_lowering=False, debug=False)

    xT = nc.dram_tensor("xT", [D, TOK], F32, kind="ExternalInput").ap()
    pA = nc.dram_tensor("pA", [D, TOK], F32, kind="ExternalInput").ap()
    pB = nc.dram_tensor("pB", [D, TOK], F32, kind="ExternalInput").ap()
    b_out = nc.dram_tensor("b_out", [128, NDCH], F32, kind="ExternalInput").ap()
    gw = nc.dram_tensor("gw", [NICH, NDCH, 128, 128], I32, kind="ExternalInput").ap()
    uw = nc.dram_tensor("uw", [NICH, NDCH, 128, 128], I32, kind="ExternalInput").ap()
    dw = nc.dram_tensor("dw", [NICH, NDCH, 128, 128], I32, kind="ExternalInput").ap()
    gsc = nc.dram_tensor("gsc", [128, NICH], F32, kind="ExternalInput").ap()
    usc = nc.dram_tensor("usc", [128, NICH], F32, kind="ExternalInput").ap()
    dsc = nc.dram_tensor("dsc", [128, NDCH], F32, kind="ExternalInput").ap()
    crow = nc.dram_tensor("crow", [4, 128], F32, kind="ExternalInput").ap()
    ccol = nc.dram_tensor("ccol", [128, 4], F32, kind="ExternalInput").ap()

    x2T = nc.dram_tensor("x2T", [D, TOK], F32, kind="ExternalOutput").ap()
    x1d = nc.dram_tensor("x1d", [D, TOK], F32, kind="Internal").ap()

    es = ExitStack()
    SB = lambda name, shape, dt: es.enter_context(nc.sbuf_tensor(name, shape, dt))
    PS = lambda name, shape: es.enter_context(nc.psum_tensor(name, shape, F32))

    ccols = SB("ccols", [128, 4], F32)
    ones_col = SB("ones_col", [128, 1], BF16)
    onesf_row = SB("onesf_row", [1, 128], F32)
    bcols = SB("bcols", [128, NDCH], F32)
    gscol = SB("gscol", [128, NICH], F32)
    uscol = SB("uscol", [128, NICH], F32)
    dscol = SB("dscol", [128, NDCH], F32)
    xbf = [SB(f"xbf{i}", [128, TOK], BF16) for i in range(NDCH)]

    sems = mk_sems(es, nc, [
        "c_v", "c_s", "c_g", "mm", "ev", "hh", "ph",
        "dcst", "dx0", "dx1", "sx1", "drs0", "drs1",
        "dg0", "dg1", "du0", "du1", "dd0", "dd1", "dxr", "sx2", "ddbg"])
    c_v, c_s, c_g, mm, ev, hh, ph = (sems[k] for k in ["c_v", "c_s", "c_g", "mm", "ev", "hh", "ph"])
    dcst, sx1, dxr, sx2, ddbg = (sems[k] for k in ["dcst", "sx1", "dxr", "sx2", "ddbg"])
    dx = [sems["dx0"], sems["dx1"]]
    drs = [sems["drs0"], sems["drs1"]]
    dg = [sems["dg0"], sems["dg1"]]
    du = [sems["du0"], sems["du1"]]
    dd = [sems["dd0"], sems["dd1"]]


    p = Plan(nc)

    p.dma("sync", ccols[:], ccol, dcst)
    p.dma("sync", onesf_row[:], crow[0:1, :], dcst)
    p.dma("sync", bcols[:], b_out, dcst)
    p.dma("sync", gscol[:], gsc, dcst)
    p.dma("sync", uscol[:], usc, dcst)
    v_const = p.dma("sync", dscol[:], dsc, dcst)

    @p.on("vector")
    def _(v):
        v.wait_ge(dcst.h, v_const)
        v.tensor_copy(out=ones_col[:], in_=ccols[:, 0:1])

    with ExitStack() as es1:
        SB1 = lambda name, shape, dt: es1.enter_context(nc.sbuf_tensor(name, shape, dt))
        PS1 = lambda name, shape: es1.enter_context(nc.psum_tensor(name, shape, F32))
        ld = [SB1(f"ld{i}", [128, TOK], F32) for i in range(6)]
        s_sq = [SB1(f"s_sq{i}", [128, TOK], BF16) for i in range(2)]
        mb_b = SB1("mb_b", [128, TOK], F32)
        rb_b = SB1("rb_b", [128, TOK], F32)
        sumx = SB1("sumx", [1, TOK], F32)
        sumsq = SB1("sumsq", [1, TOK], F32)
        scr_row = SB1("scr_row", [1, 512], F32)
        eps_t = SB1("eps_t", [1, 1], F32)
        mrow, vrow = sumx, sumsq
        ps_sx = [PS1(f"ps_sx{i}", [1, 512]) for i in range(2)]
        ps_sq = [PS1(f"ps_sq{i}", [1, 512]) for i in range(2)]
        ps_bc = PS1("ps_bc", [128, 512])

        @p.on("vector")
        def _(v):
            v.memset(sumx[:], 0.0)
            v.memset(sumsq[:], 0.0)
            v.memset(eps_t[:], 1e-5)

        x1_marks = []
        for dch in range(NDCH):
            sl2 = dch % 2
            bi = sl2 * 3
            lw = []
            if dch >= 2:
                lw = [(c_v, x1_marks[dch - 2]), (sx1, 16 * (dch - 1))]
            vq = 48 * (dch // 2 + 1)
            p.dma("sync", ld[bi][:], xT[dch * 128:(dch + 1) * 128, :], dx[sl2], waits=lw)
            p.dma("sync", ld[bi + 1][:], pA[dch * 128:(dch + 1) * 128, :], dx[sl2])
            p.dma("sync", ld[bi + 2][:], pB[dch * 128:(dch + 1) * 128, :], dx[sl2])

            @p.on("vector")
            def _(v, dch=dch, bi=bi, sl2=sl2, vq=vq, wmm=mm.n):
                v.wait_ge(dx[sl2].h, vq)
                if dch >= 2:
                    v.wait_ge(mm.h, wmm)
                v.tensor_tensor(out=ld[bi][:], in0=ld[bi][:], in1=ld[bi + 1][:], op=ALU.add)
                v.tensor_tensor(out=ld[bi][:], in0=ld[bi][:], in1=ld[bi + 2][:], op=ALU.add)
                v.tensor_scalar(out=ld[bi][:], in0=ld[bi][:],
                                scalar1=bcols[:, dch:dch + 1], scalar2=None, op0=ALU.add)
                v.tensor_copy(out=xbf[dch][:], in_=ld[bi][:])
                v.tensor_tensor(out=s_sq[sl2][:], in0=xbf[dch][:], in1=xbf[dch][:],
                                op=ALU.mult).then_inc(c_v.h, 1)
            c_x1 = c_v.inc()
            x1_marks.append(c_x1)
            p.dma("gpsimd", x1d[dch * 128:(dch + 1) * 128, :], ld[bi][:], sx1,
                  waits=[(c_v, c_x1)])

            for su in range(4):
                @p.on("tensor")
                def _(t, dch=dch, su=su, sl2=sl2, c_x1=c_x1, wev=ev.n):
                    t.wait_ge(c_v.h, c_x1)
                    if wev > 1:
                        t.wait_ge(ev.h, wev - 1)
                    t.matmul(out=ps_sx[su % 2][:], lhsT=ones_col[:],
                             rhs=xbf[dch][:, su * 512:(su + 1) * 512], start=True, stop=True)
                    t.matmul(out=ps_sq[su % 2][:], lhsT=ones_col[:],
                             rhs=s_sq[sl2][:, su * 512:(su + 1) * 512],
                             start=True, stop=True).then_inc(mm.h, 1)
                c_mm = mm.inc()

                @p.on("vector")
                def _(v, su=su, c_mm=c_mm):
                    v.wait_ge(mm.h, c_mm)
                    sl = slice(su * 512, (su + 1) * 512)
                    v.tensor_tensor(out=sumx[:, sl], in0=sumx[:, sl], in1=ps_sx[su % 2][:], op=ALU.add)
                    v.tensor_tensor(out=sumsq[:, sl], in0=sumsq[:, sl], in1=ps_sq[su % 2][:],
                                    op=ALU.add).then_inc(ev.h, 1)
                ev.inc()

        st_x1 = sx1.n

        @p.on("vector")
        def _(v, wev=ev.n):
            v.wait_ge(ev.h, wev)
            v.tensor_scalar(out=mrow[:], in0=sumx[:], scalar1=1.0 / D, scalar2=None, op0=ALU.mult)
            v.tensor_scalar(out=vrow[:], in0=sumsq[:], scalar1=1.0 / D, scalar2=None, op0=ALU.mult)
            for su in range(4):
                sl = slice(su * 512, (su + 1) * 512)
                v.tensor_tensor(out=scr_row[:], in0=mrow[:, sl], in1=mrow[:, sl], op=ALU.mult)
                ti = v.tensor_tensor(out=vrow[:, sl], in0=vrow[:, sl], in1=scr_row[:], op=ALU.subtract)
            ti.then_inc(ev.h, 1)
        c_var = ev.inc()

        @p.on("scalar")
        def _(sc, c_var=c_var):
            sc.wait_ge(ev.h, c_var)
            sc.activation(out=vrow[:], in_=vrow[:], func=AF.Ln, bias=eps_t[:])
            sc.activation(out=vrow[:], in_=vrow[:], func=AF.Exp, scale=-0.5).then_inc(c_s.h, 1)
        c_rstd = c_s.inc()

        for (row, dst) in ((mrow, mb_b), (vrow, rb_b)):
            for su in range(4):
                @p.on("tensor")
                def _(t, row=row, su=su, c_rstd=c_rstd, wev=ev.n):
                    t.wait_ge(c_s.h, c_rstd)
                    t.wait_ge(ev.h, wev)
                    t.matmul(out=ps_bc[:], lhsT=onesf_row[:], rhs=row[:, su * 512:(su + 1) * 512],
                             start=True, stop=True).then_inc(mm.h, 1)
                c1 = mm.inc()
                @p.on("vector")
                def _(v, dst=dst, su=su, c1=c1):
                    v.wait_ge(mm.h, c1)
                    v.tensor_copy(out=dst[:, su * 512:(su + 1) * 512], in_=ps_bc[:]).then_inc(ev.h, 1)
                ev.inc()

        c_bcast = ev.n
        norm_marks = []
        for dch in range(NDCH):
            sl2 = dch % 2
            bi = sl2
            waits = ([(sx1, st_x1)] if dch < 2 else
                     [(ev, norm_marks[dch - 2])])
            vx = 16 * (dch // 2 + 1)
            p.dma("sync", ld[bi][:], x1d[dch * 128:(dch + 1) * 128, :], drs[sl2], waits=waits)
            @p.on("vector")
            def _(v, dch=dch, bi=bi, sl2=sl2, vx=vx, c_bcast=c_bcast):
                v.wait_ge(drs[sl2].h, vx)
                v.wait_ge(ev.h, c_bcast)
                v.tensor_tensor(out=ld[bi][:], in0=ld[bi][:], in1=mb_b[:], op=ALU.subtract)
                v.tensor_tensor(out=ld[bi][:], in0=ld[bi][:], in1=rb_b[:], op=ALU.mult)
                v.tensor_copy(out=xbf[dch][:], in_=ld[bi][:]).then_inc(ev.h, 1)
            norm_marks.append(ev.inc())
        dbg_n = ddbg.n

        @p.on("vector")
        def _(v):
            v.tensor_copy(out=scr_row[0:1, 0:1], in_=ld[1][0:1, 0:1]).then_inc(ph.h, 1)
        ph.inc()

    SBLK = 4
    NSB = NICH // SBLK
    HTOK = TOK // 2

    stg = [SB(f"stg{i}", [128, 2048], I32) for i in range(2)]
    gbf = [SB(f"gbf{i}", [128, 2048], BF16) for i in range(2)]
    ubf = [SB(f"ubf{i}", [128, 2048], BF16) for i in range(2)]
    dbf = [SB(f"dbf{i}", [128, 2048], BF16) for i in range(SBLK)]
    hblk = [SB(f"hblk{i}", [128, HTOK], BF16) for i in range(SBLK)]
    dacc = [SB(f"dacc{i}", [128, HTOK], F32) for i in range(NDCH)]
    s_silu = [SB(f"s_silu{i}", [128, 512], BF16) for i in range(2)]
    u_sb = [SB(f"u_sb{i}", [128, 512], BF16) for i in range(2)]
    x1rd = SB("x1rd", [128, HTOK], F32)
    x2st = SB("x2st", [128, HTOK], F32)

    ps_g = [PS(f"ps_g{i}", [128, 512]) for i in range(2)]
    ps_u = [PS(f"ps_u{i}", [128, 512]) for i in range(2)]
    ps_d = [PS(f"ps_d{i}", [128, 512]) for i in range(2)]

    for eng in ("tensor", "scalar", "gpsimd", "sync"):
        @p.on(eng)
        def _(e):
            e.wait_ge(ph.h, 1)

    last_h = {0: 0, 1: 0}
    stg_mark = [(None, 0), (None, 0)]
    last_dcvt = 0
    last_gu_mm = {0: 0, 1: 0}
    last_dev = {0: 0, 1: 0}
    sb_dn_mark = 0

    step = 0
    for th in range(2):
        tok0 = th * HTOK
        @p.on("vector")
        def _(v, wst=sx2.n):
            v.wait_ge(ph.h, 1)
            if wst > 0:
                v.wait_ge(sx2.h, wst)
            for hdch in range(NDCH):
                mi = v.memset(dacc[hdch][:], 0.0)
            mi.then_inc(ev.h, 1)
        c_z = ev.inc()

        for sb in range(NSB):
            for k in range(SBLK):
                ich = sb * SBLK + k
                kp = ich % 2
                q = ich % 2
                ws = stg_mark[q]
                vg = p.dma("sync", stg[q][:].rearrange("p (a c) -> p a c", a=16),
                           gw[ich].rearrange("a p c -> p a c"), dg[q],
                           waits=[ws] if ws[0] else [])
                @p.on("scalar")
                def _(sc, vg=vg, kp=kp, q=q, wmm=last_gu_mm[kp]):
                    sc.wait_ge(dg[q].h, vg)
                    if wmm > 0:
                        sc.wait_ge(mm.h, wmm)
                    sc.activation(out=gbf[kp][:], in_=stg[q][:], func=AF.Copy,
                                  bias=-128.0).then_inc(c_s.h, 1)
                c_gcvt = c_s.inc()

                ws = stg_mark[1 - q]
                vu = p.dma("scalar", stg[1 - q][:].rearrange("p (a c) -> p a c", a=16),
                           uw[ich].rearrange("a p c -> p a c"), du[1 - q],
                           waits=[ws] if ws[0] else [])
                @p.on("scalar")
                def _(sc, vu=vu, kp=kp, q=q, wmm=last_gu_mm[kp]):
                    sc.wait_ge(du[1 - q].h, vu)
                    if wmm > 0:
                        sc.wait_ge(mm.h, wmm)
                    sc.activation(out=ubf[kp][:], in_=stg[1 - q][:], func=AF.Copy,
                                  bias=-128.0).then_inc(c_s.h, 1)
                c_ucvt = c_s.inc()
                stg_mark[1 - q] = (c_s, c_ucvt)

                vd = p.dma("gpsimd", stg[q][:].rearrange("p (a c) -> p a c", a=16),
                           dw[ich].rearrange("a p c -> p a c"), dd[q],
                           waits=[(c_s, c_gcvt)])
                @p.on("scalar")
                def _(sc, vd=vd, k=k, q=q, wdn=sb_dn_mark):
                    sc.wait_ge(dd[q].h, vd)
                    if wdn > 0:
                        sc.wait_ge(mm.h, wdn)
                    sc.activation(out=dbf[k][:], in_=stg[q][:], func=AF.Copy,
                                  bias=-128.0).then_inc(c_s.h, 1)
                last_dcvt = c_s.inc()
                stg_mark[q] = (c_s, last_dcvt)

                for tsub in range(2):
                    pp = step % 2
                    sl = slice(tok0 + tsub * 512, tok0 + (tsub + 1) * 512)
                    @p.on("tensor")
                    def _(t, c_gcvt=c_gcvt, c_ucvt=c_ucvt, kp=kp, pp=pp, sl=sl,
                          whh=last_h[pp], c_z=c_z):
                        t.wait_ge(c_s.h, c_gcvt)
                        t.wait_ge(c_s.h, c_ucvt)
                        t.wait_ge(ev.h, c_z)
                        if whh > 0:
                            t.wait_ge(hh.h, whh)
                        for dch in range(NDCH):
                            t.matmul(out=ps_g[pp][:], lhsT=gbf[kp][:, dch * 128:(dch + 1) * 128],
                                     rhs=xbf[dch][:, sl], start=(dch == 0), stop=(dch == NDCH - 1))
                        for dch in range(NDCH):
                            mmi = t.matmul(out=ps_u[pp][:], lhsT=ubf[kp][:, dch * 128:(dch + 1) * 128],
                                           rhs=xbf[dch][:, sl], start=(dch == 0), stop=(dch == NDCH - 1))
                        mmi.then_inc(mm.h, 1)
                    c_gu = mm.inc()
                    last_gu_mm[kp] = c_gu

                    @p.on("scalar")
                    def _(sc, c_gu=c_gu, pp=pp, ich=ich):
                        sc.wait_ge(mm.h, c_gu)
                        sc.activation(out=s_silu[pp][:], in_=ps_g[pp][:], func=AF.Silu,
                                      scale=gscol[:, ich:ich + 1]).then_inc(c_s.h, 1)
                    c_si = c_s.inc()

                    @p.on("vector")
                    def _(v, c_si=c_si, pp=pp, k=k, tsub=tsub, ich=ich, wdn=sb_dn_mark):
                        v.wait_ge(c_s.h, c_si)
                        if wdn > 0:
                            v.wait_ge(mm.h, wdn)
                        v.tensor_scalar(out=u_sb[pp][:], in0=ps_u[pp][:], scalar1=uscol[:, ich:ich + 1],
                                        scalar2=None, op0=ALU.mult)
                        v.tensor_tensor(out=hblk[k][:, tsub * 512:(tsub + 1) * 512],
                                        in0=s_silu[pp][:], in1=u_sb[pp][:], op=ALU.mult).then_inc(hh.h, 1)
                    last_h[pp] = hh.inc()
                    step += 1

            all_h = hh.n
            for hdch in range(NDCH):
                for tsub in range(2):
                    dp = (hdch * 2 + tsub) % 2
                    @p.on("tensor")
                    def _(t, hdch=hdch, tsub=tsub, dp=dp, c_d=last_dcvt, all_h=all_h,
                          wdev=last_dev[dp]):
                        t.wait_ge(c_s.h, c_d)
                        t.wait_ge(hh.h, all_h)
                        if wdev > 0:
                            t.wait_ge(ev.h, wdev)
                        for k in range(SBLK):
                            mmi = t.matmul(out=ps_d[dp][:], lhsT=dbf[k][:, hdch * 128:(hdch + 1) * 128],
                                           rhs=hblk[k][:, tsub * 512:(tsub + 1) * 512],
                                           start=(k == 0), stop=(k == SBLK - 1))
                        mmi.then_inc(mm.h, 1)
                    c_dn = mm.inc()
                    @p.on("vector")
                    def _(v, hdch=hdch, tsub=tsub, dp=dp, c_dn=c_dn):
                        v.wait_ge(mm.h, c_dn)
                        sl = slice(tsub * 512, (tsub + 1) * 512)
                        v.tensor_tensor(out=dacc[hdch][:, sl], in0=dacc[hdch][:, sl],
                                        in1=ps_d[dp][:], op=ALU.add).then_inc(ev.h, 1)
                    last_dev[dp] = ev.inc()
            sb_dn_mark = mm.n

        for hdch in range(NDCH):
            gi = th * NDCH + hdch
            p.dma("sync", x1rd[:], x1d[hdch * 128:(hdch + 1) * 128, tok0:tok0 + HTOK],
                  dxr, waits=[(ev, ev.n)])
            vx = 16 * (gi + 1)
            @p.on("vector")
            def _(v, hdch=hdch, vx=vx, wev=ev.n, wst=sx2.n):
                v.wait_ge(ev.h, wev)
                v.wait_ge(dxr.h, vx)
                if wst > 0:
                    v.wait_ge(sx2.h, wst)
                v.scalar_tensor_tensor(out=x2st[:], in0=dacc[hdch][:],
                                       scalar=dscol[:, hdch:hdch + 1], in1=x1rd[:],
                                       op0=ALU.mult, op1=ALU.add).then_inc(ev.h, 1)
            c_cmb = ev.inc()
            p.dma("gpsimd", x2T[hdch * 128:(hdch + 1) * 128, tok0:tok0 + HTOK],
                  x2st[:], sx2, waits=[(ev, c_cmb)])

    @p.on("gpsimd")
    def _(g, f1=sx1.n, f2=sx2.n, f3=ddbg.n):
        g.wait_ge(sx1.h, f1)
        g.wait_ge(sx2.h, f2)
        g.wait_ge(ddbg.h, f3)

    p.emit()
    es.close()
    return nc



def _consts():
    crow = np.zeros((4, 128), np.float32)
    crow[0] = 1.0
    crow[1] = -128.0
    ccol = np.zeros((128, 4), np.float32)
    ccol[:, 0] = 1.0
    ccol[:, 1] = -128.0
    return crow, ccol


def tile_w_ffn_up(w_t):
    return np.ascontiguousarray(w_t.reshape(NDCH, 128, NICH, 128).transpose(2, 0, 1, 3))


def tile_w_ffn_down(w_t):
    return np.ascontiguousarray(w_t.reshape(NICH, 128, NDCH, 128).transpose(0, 2, 1, 3))


def prep_B_core(x_sl_T, pA_sl, pB_sl, inputs, wcache):
    if "gw" not in wcache:
        wcache["gw"] = tile_w_ffn_up(np.ascontiguousarray(inputs["L0_ffn_gate_w"].T))
        wcache["uw"] = tile_w_ffn_up(np.ascontiguousarray(inputs["L0_ffn_up_w"].T))
        wcache["dw"] = tile_w_ffn_down(np.ascontiguousarray(inputs["L0_ffn_down_w"].T))
    crow, ccol = _consts()
    return {
        "xT": x_sl_T, "pA": pA_sl, "pB": pB_sl,
        "b_out": np.ascontiguousarray(inputs["L0_attn_b_out"].reshape(NDCH, 128).T),
        "gw": wcache["gw"], "uw": wcache["uw"], "dw": wcache["dw"],
        "gsc": np.ascontiguousarray(inputs["L0_ffn_gate_scale"].reshape(NICH, 128).T),
        "usc": np.ascontiguousarray(inputs["L0_ffn_up_scale"].reshape(NICH, 128).T),
        "dsc": np.ascontiguousarray(inputs["L0_ffn_down_scale"].reshape(NDCH, 128).T),
        "crow": crow, "ccol": ccol,
    }


def run_B(inputs, x, attn_out):
    nc = build_B()
    in_maps = []
    wcache = {}
    for c in range(8):
        b, thf = c // 2, c % 2
        tsl = slice(thf * TOK, (thf + 1) * TOK)
        xT = np.ascontiguousarray(x[b, tsl].T)
        pa = np.ascontiguousarray(attn_out[b, tsl].T)
        pb = np.zeros_like(pa)
        in_maps.append(prep_B_core(xT, pa, pb, inputs, wcache))
    res = run_bass_kernel_spmd(nc, in_maps, core_ids=list(range(8)))
    x2 = np.zeros((4, S, D), np.float32)
    for c in range(8):
        b, thf = c // 2, c % 2
        x2[b, thf * TOK:(thf + 1) * TOK] = res.results[c]["x2T"].T
    return x2


def build_D():
    nc = bass.Bass("TRN2", target_bir_lowering=False, debug=False)

    xT = nc.dram_tensor("xT", [D, TOK], F32, kind="ExternalInput").ap()
    pA = nc.dram_tensor("pA", [D, TOK], F32, kind="ExternalInput").ap()
    pB = nc.dram_tensor("pB", [D, TOK], F32, kind="ExternalInput").ap()
    b_out = nc.dram_tensor("b_out", [128, NDCH], F32, kind="ExternalInput").ap()
    gw = nc.dram_tensor("gw", [NICH, NDCH, 128, 128], F32, kind="ExternalInput").ap()
    uw = nc.dram_tensor("uw", [NICH, NDCH, 128, 128], F32, kind="ExternalInput").ap()
    dw = nc.dram_tensor("dw", [NICH, NDCH, 128, 128], F32, kind="ExternalInput").ap()
    crow = nc.dram_tensor("crow", [4, 128], F32, kind="ExternalInput").ap()
    ccol = nc.dram_tensor("ccol", [128, 4], F32, kind="ExternalInput").ap()

    x2T = nc.dram_tensor("x4T", [D, TOK], F32, kind="ExternalOutput").ap()
    x1d = nc.dram_tensor("x3d", [D, TOK], F32, kind="Internal").ap()

    es = ExitStack()
    SB = lambda name, shape, dt: es.enter_context(nc.sbuf_tensor(name, shape, dt))
    PS = lambda name, shape: es.enter_context(nc.psum_tensor(name, shape, F32))

    ccols = SB("ccols", [128, 4], F32)
    ones_col = SB("ones_col", [128, 1], BF16)
    onesf_row = SB("onesf_row", [1, 128], F32)
    bcols = SB("bcols", [128, NDCH], F32)
    xbf = [SB(f"xbf{i}", [128, TOK], BF16) for i in range(NDCH)]

    sems = mk_sems(es, nc, [
        "c_v", "c_s", "c_g", "mm", "ev", "hh", "ph",
        "dcst", "dx0", "dx1", "sx1", "drs0", "drs1",
        "dg0", "dg1", "du0", "du1", "dd0", "dd1", "dxr", "sx2", "ddbg"])
    c_v, c_s, c_g, mm, ev, hh, ph = (sems[k] for k in ["c_v", "c_s", "c_g", "mm", "ev", "hh", "ph"])
    dcst, sx1, dxr, sx2, ddbg = (sems[k] for k in ["dcst", "sx1", "dxr", "sx2", "ddbg"])
    dx = [sems["dx0"], sems["dx1"]]
    drs = [sems["drs0"], sems["drs1"]]
    dg = [sems["dg0"], sems["dg1"]]
    du = [sems["du0"], sems["du1"]]
    dd = [sems["dd0"], sems["dd1"]]


    p = Plan(nc)

    p.dma("sync", ccols[:], ccol, dcst)
    p.dma("sync", onesf_row[:], crow[0:1, :], dcst)
    v_const = p.dma("sync", bcols[:], b_out, dcst)

    @p.on("vector")
    def _(v):
        v.wait_ge(dcst.h, v_const)
        v.tensor_copy(out=ones_col[:], in_=ccols[:, 0:1])

    with ExitStack() as es1:
        SB1 = lambda name, shape, dt: es1.enter_context(nc.sbuf_tensor(name, shape, dt))
        PS1 = lambda name, shape: es1.enter_context(nc.psum_tensor(name, shape, F32))
        ld = [SB1(f"ld{i}", [128, TOK], F32) for i in range(6)]
        s_sq = [SB1(f"s_sq{i}", [128, TOK], BF16) for i in range(2)]
        mb_b = SB1("mb_b", [128, TOK], F32)
        rb_b = SB1("rb_b", [128, TOK], F32)
        sumx = SB1("sumx", [1, TOK], F32)
        sumsq = SB1("sumsq", [1, TOK], F32)
        scr_row = SB1("scr_row", [1, 512], F32)
        eps_t = SB1("eps_t", [1, 1], F32)
        mrow, vrow = sumx, sumsq
        ps_sx = [PS1(f"ps_sx{i}", [1, 512]) for i in range(2)]
        ps_sq = [PS1(f"ps_sq{i}", [1, 512]) for i in range(2)]
        ps_bc = PS1("ps_bc", [128, 512])

        @p.on("vector")
        def _(v):
            v.memset(sumx[:], 0.0)
            v.memset(sumsq[:], 0.0)
            v.memset(eps_t[:], 1e-5)

        x1_marks = []
        for dch in range(NDCH):
            sl2 = dch % 2
            bi = sl2 * 3
            lw = []
            if dch >= 2:
                lw = [(c_v, x1_marks[dch - 2]), (sx1, 16 * (dch - 1))]
            vq = 48 * (dch // 2 + 1)
            p.dma("sync", ld[bi][:], xT[dch * 128:(dch + 1) * 128, :], dx[sl2], waits=lw)
            p.dma("sync", ld[bi + 1][:], pA[dch * 128:(dch + 1) * 128, :], dx[sl2])
            p.dma("sync", ld[bi + 2][:], pB[dch * 128:(dch + 1) * 128, :], dx[sl2])

            @p.on("vector")
            def _(v, dch=dch, bi=bi, sl2=sl2, vq=vq, wmm=mm.n):
                v.wait_ge(dx[sl2].h, vq)
                if dch >= 2:
                    v.wait_ge(mm.h, wmm)
                v.tensor_tensor(out=ld[bi][:], in0=ld[bi][:], in1=ld[bi + 1][:], op=ALU.add)
                v.tensor_tensor(out=ld[bi][:], in0=ld[bi][:], in1=ld[bi + 2][:], op=ALU.add)
                v.tensor_scalar(out=ld[bi][:], in0=ld[bi][:],
                                scalar1=bcols[:, dch:dch + 1], scalar2=None, op0=ALU.add)
                v.tensor_copy(out=xbf[dch][:], in_=ld[bi][:])
                v.tensor_tensor(out=s_sq[sl2][:], in0=xbf[dch][:], in1=xbf[dch][:],
                                op=ALU.mult).then_inc(c_v.h, 1)
            c_x1 = c_v.inc()
            x1_marks.append(c_x1)
            p.dma("gpsimd", x1d[dch * 128:(dch + 1) * 128, :], ld[bi][:], sx1,
                  waits=[(c_v, c_x1)])

            for su in range(4):
                @p.on("tensor")
                def _(t, dch=dch, su=su, sl2=sl2, c_x1=c_x1, wev=ev.n):
                    t.wait_ge(c_v.h, c_x1)
                    if wev > 1:
                        t.wait_ge(ev.h, wev - 1)
                    t.matmul(out=ps_sx[su % 2][:], lhsT=ones_col[:],
                             rhs=xbf[dch][:, su * 512:(su + 1) * 512], start=True, stop=True)
                    t.matmul(out=ps_sq[su % 2][:], lhsT=ones_col[:],
                             rhs=s_sq[sl2][:, su * 512:(su + 1) * 512],
                             start=True, stop=True).then_inc(mm.h, 1)
                c_mm = mm.inc()

                @p.on("vector")
                def _(v, su=su, c_mm=c_mm):
                    v.wait_ge(mm.h, c_mm)
                    sl = slice(su * 512, (su + 1) * 512)
                    v.tensor_tensor(out=sumx[:, sl], in0=sumx[:, sl], in1=ps_sx[su % 2][:], op=ALU.add)
                    v.tensor_tensor(out=sumsq[:, sl], in0=sumsq[:, sl], in1=ps_sq[su % 2][:],
                                    op=ALU.add).then_inc(ev.h, 1)
                ev.inc()

        st_x1 = sx1.n

        @p.on("vector")
        def _(v, wev=ev.n):
            v.wait_ge(ev.h, wev)
            v.tensor_scalar(out=mrow[:], in0=sumx[:], scalar1=1.0 / D, scalar2=None, op0=ALU.mult)
            v.tensor_scalar(out=vrow[:], in0=sumsq[:], scalar1=1.0 / D, scalar2=None, op0=ALU.mult)
            for su in range(4):
                sl = slice(su * 512, (su + 1) * 512)
                v.tensor_tensor(out=scr_row[:], in0=mrow[:, sl], in1=mrow[:, sl], op=ALU.mult)
                ti = v.tensor_tensor(out=vrow[:, sl], in0=vrow[:, sl], in1=scr_row[:], op=ALU.subtract)
            ti.then_inc(ev.h, 1)
        c_var = ev.inc()

        @p.on("scalar")
        def _(sc, c_var=c_var):
            sc.wait_ge(ev.h, c_var)
            sc.activation(out=vrow[:], in_=vrow[:], func=AF.Ln, bias=eps_t[:])
            sc.activation(out=vrow[:], in_=vrow[:], func=AF.Exp, scale=-0.5).then_inc(c_s.h, 1)
        c_rstd = c_s.inc()

        for (row, dst) in ((mrow, mb_b), (vrow, rb_b)):
            for su in range(4):
                @p.on("tensor")
                def _(t, row=row, su=su, c_rstd=c_rstd, wev=ev.n):
                    t.wait_ge(c_s.h, c_rstd)
                    t.wait_ge(ev.h, wev)
                    t.matmul(out=ps_bc[:], lhsT=onesf_row[:], rhs=row[:, su * 512:(su + 1) * 512],
                             start=True, stop=True).then_inc(mm.h, 1)
                c1 = mm.inc()
                @p.on("vector")
                def _(v, dst=dst, su=su, c1=c1):
                    v.wait_ge(mm.h, c1)
                    v.tensor_copy(out=dst[:, su * 512:(su + 1) * 512], in_=ps_bc[:]).then_inc(ev.h, 1)
                ev.inc()

        c_bcast = ev.n
        norm_marks = []
        for dch in range(NDCH):
            sl2 = dch % 2
            bi = sl2
            waits = ([(sx1, st_x1)] if dch < 2 else
                     [(ev, norm_marks[dch - 2])])
            vx = 16 * (dch // 2 + 1)
            p.dma("sync", ld[bi][:], x1d[dch * 128:(dch + 1) * 128, :], drs[sl2], waits=waits)
            @p.on("vector")
            def _(v, dch=dch, bi=bi, sl2=sl2, vx=vx, c_bcast=c_bcast):
                v.wait_ge(drs[sl2].h, vx)
                v.wait_ge(ev.h, c_bcast)
                v.tensor_tensor(out=ld[bi][:], in0=ld[bi][:], in1=mb_b[:], op=ALU.subtract)
                v.tensor_tensor(out=ld[bi][:], in0=ld[bi][:], in1=rb_b[:], op=ALU.mult)
                v.tensor_copy(out=xbf[dch][:], in_=ld[bi][:]).then_inc(ev.h, 1)
            norm_marks.append(ev.inc())
        dbg_n = ddbg.n

        @p.on("vector")
        def _(v):
            v.tensor_copy(out=scr_row[0:1, 0:1], in_=ld[1][0:1, 0:1]).then_inc(ph.h, 1)
        ph.inc()

    SBLK = 4
    NSB = NICH // SBLK
    HTOK = TOK // 2

    stg = [SB(f"stg{i}", [128, 2048], F32) for i in range(2)]
    gbf = [SB(f"gbf{i}", [128, 2048], BF16) for i in range(2)]
    ubf = [SB(f"ubf{i}", [128, 2048], BF16) for i in range(2)]
    dbf = [SB(f"dbf{i}", [128, 2048], BF16) for i in range(SBLK)]
    hblk = [SB(f"hblk{i}", [128, HTOK], BF16) for i in range(SBLK)]
    dacc = [SB(f"dacc{i}", [128, HTOK], F32) for i in range(NDCH)]
    s_silu = [SB(f"s_silu{i}", [128, 512], BF16) for i in range(2)]
    s_abs = [SB(f"s_abs{i}", [128, 512], BF16) for i in range(SBLK)]
    s_m1 = SB("s_m1", [128, 512], BF16)
    s_n1 = SB("s_n1", [128, 512], BF16)
    s_m2 = SB("s_m2", [128, 512], BF16)
    s_n2 = SB("s_n2", [128, 512], BF16)
    u_sb = [SB(f"u_sb{i}", [128, 512], BF16) for i in range(2)]
    x1rd = SB("x1rd", [128, HTOK], F32)
    x2st = SB("x2st", [128, HTOK], F32)

    ps_g = [PS(f"ps_g{i}", [128, 512]) for i in range(2)]
    ps_u = [PS(f"ps_u{i}", [128, 512]) for i in range(2)]
    ps_d = [PS(f"ps_d{i}", [128, 512]) for i in range(2)]

    for eng in ("tensor", "scalar", "gpsimd", "sync"):
        @p.on(eng)
        def _(e):
            e.wait_ge(ph.h, 1)

    last_h = {0: 0, 1: 0}
    stg_mark = [(None, 0), (None, 0)]
    last_dcvt = 0
    last_gu_mm = {0: 0, 1: 0}
    last_dev = {0: 0, 1: 0}
    sb_dn_mark = 0

    step = 0
    for th in range(2):
        tok0 = th * HTOK
        @p.on("vector")
        def _(v, wst=sx2.n):
            v.wait_ge(ph.h, 1)
            if wst > 0:
                v.wait_ge(sx2.h, wst)
            for hdch in range(NDCH):
                mi = v.memset(dacc[hdch][:], 0.0)
            mi.then_inc(ev.h, 1)
        c_z = ev.inc()

        for sb in range(NSB):
            for k in range(SBLK):
                ich = k * 16 + sb
                kp = ich % 2
                q = ich % 2
                ws = stg_mark[q]
                vg = p.dma("sync", stg[q][:].rearrange("p (a c) -> p a c", a=16),
                           gw[ich].rearrange("a p c -> p a c"), dg[q],
                           waits=[ws] if ws[0] else [])
                @p.on("scalar")
                def _(sc, vg=vg, kp=kp, q=q, wmm=last_gu_mm[kp]):
                    sc.wait_ge(dg[q].h, vg)
                    if wmm > 0:
                        sc.wait_ge(mm.h, wmm)
                    sc.copy(out=gbf[kp][:], in_=stg[q][:]).then_inc(c_s.h, 1)
                c_gcvt = c_s.inc()

                ws = stg_mark[1 - q]
                vu = p.dma("scalar", stg[1 - q][:].rearrange("p (a c) -> p a c", a=16),
                           uw[ich].rearrange("a p c -> p a c"), du[1 - q],
                           waits=[ws] if ws[0] else [])
                @p.on("scalar")
                def _(sc, vu=vu, kp=kp, q=q, wmm=last_gu_mm[kp]):
                    sc.wait_ge(du[1 - q].h, vu)
                    if wmm > 0:
                        sc.wait_ge(mm.h, wmm)
                    sc.copy(out=ubf[kp][:], in_=stg[1 - q][:]).then_inc(c_s.h, 1)
                c_ucvt = c_s.inc()
                stg_mark[1 - q] = (c_s, c_ucvt)

                vd = p.dma("gpsimd", stg[q][:].rearrange("p (a c) -> p a c", a=16),
                           dw[ich].rearrange("a p c -> p a c"), dd[q],
                           waits=[(c_s, c_gcvt)])
                @p.on("scalar")
                def _(sc, vd=vd, k=k, q=q, wdn=sb_dn_mark):
                    sc.wait_ge(dd[q].h, vd)
                    if wdn > 0:
                        sc.wait_ge(mm.h, wdn)
                    sc.copy(out=dbf[k][:], in_=stg[q][:]).then_inc(c_s.h, 1)
                last_dcvt = c_s.inc()
                stg_mark[q] = (c_s, last_dcvt)

                for tsub in range(2):
                    pp = step % 2
                    sl = slice(tok0 + tsub * 512, tok0 + (tsub + 1) * 512)
                    @p.on("tensor")
                    def _(t, c_gcvt=c_gcvt, c_ucvt=c_ucvt, kp=kp, pp=pp, sl=sl,
                          whh=last_h[pp], c_z=c_z):
                        t.wait_ge(c_s.h, c_gcvt)
                        t.wait_ge(c_s.h, c_ucvt)
                        t.wait_ge(ev.h, c_z)
                        if whh > 0:
                            t.wait_ge(hh.h, whh)
                        for dch in range(NDCH):
                            t.matmul(out=ps_g[pp][:], lhsT=gbf[kp][:, dch * 128:(dch + 1) * 128],
                                     rhs=xbf[dch][:, sl], start=(dch == 0), stop=(dch == NDCH - 1))
                        for dch in range(NDCH):
                            mmi = t.matmul(out=ps_u[pp][:], lhsT=ubf[kp][:, dch * 128:(dch + 1) * 128],
                                           rhs=xbf[dch][:, sl], start=(dch == 0), stop=(dch == NDCH - 1))
                        mmi.then_inc(mm.h, 1)
                    c_gu = mm.inc()
                    last_gu_mm[kp] = c_gu

                    @p.on("scalar")
                    def _(sc, c_gu=c_gu, pp=pp, ich=ich):
                        sc.wait_ge(mm.h, c_gu)
                        sc.activation(out=s_silu[pp][:], in_=ps_g[pp][:],
                                      func=AF.Silu).then_inc(c_s.h, 1)
                    c_si = c_s.inc()

                    @p.on("vector")
                    def _(v, c_si=c_si, pp=pp, k=k, tsub=tsub, ich=ich, wdn=sb_dn_mark):
                        v.wait_ge(c_s.h, c_si)
                        if wdn > 0:
                            v.wait_ge(mm.h, wdn)
                        v.tensor_copy(out=u_sb[pp][:], in_=ps_u[pp][:])
                        v.tensor_tensor(out=hblk[k][:, tsub * 512:(tsub + 1) * 512],
                                        in0=s_silu[pp][:], in1=u_sb[pp][:], op=ALU.mult).then_inc(hh.h, 1)
                    last_h[pp] = hh.inc()
                    step += 1

            all_h0 = hh.n
            for tsub in range(2):
                tsl = slice(tsub * 512, (tsub + 1) * 512)
                @p.on("scalar")
                def _(sc, tsl=tsl, whh=max(all_h0, hh.n)):
                    sc.wait_ge(hh.h, whh)
                    for k in range(SBLK):
                        ai = sc.activation(out=s_abs[k][:], in_=hblk[k][:, tsl], func=AF.Abs)
                    ai.then_inc(c_s.h, 1)
                c_abs = c_s.inc()
                @p.on("vector")
                def _(v, tsl=tsl, c_abs=c_abs):
                    v.wait_ge(c_s.h, c_abs)
                    v.tensor_tensor(out=s_m1[:], in0=s_abs[0][:], in1=s_abs[1][:], op=ALU.max)
                    v.tensor_tensor(out=s_n1[:], in0=s_abs[0][:], in1=s_abs[1][:], op=ALU.min)
                    v.tensor_tensor(out=s_m2[:], in0=s_abs[2][:], in1=s_abs[3][:], op=ALU.max)
                    v.tensor_tensor(out=s_n2[:], in0=s_abs[2][:], in1=s_abs[3][:], op=ALU.min)
                    v.tensor_tensor(out=s_m1[:], in0=s_m1[:], in1=s_m2[:], op=ALU.min)
                    v.tensor_tensor(out=s_n1[:], in0=s_n1[:], in1=s_n2[:], op=ALU.max)
                    v.tensor_tensor(out=s_m1[:], in0=s_m1[:], in1=s_n1[:], op=ALU.max)
                    for k in range(SBLK):
                        v.tensor_tensor(out=s_abs[k][:], in0=s_abs[k][:], in1=s_m1[:], op=ALU.is_ge)
                        vi = v.tensor_tensor(out=hblk[k][:, tsl], in0=hblk[k][:, tsl],
                                             in1=s_abs[k][:], op=ALU.mult)
                    vi.then_inc(hh.h, 1)
                hh.inc()
            all_h = hh.n
            for hdch in range(NDCH):
                for tsub in range(2):
                    dp = (hdch * 2 + tsub) % 2
                    @p.on("tensor")
                    def _(t, hdch=hdch, tsub=tsub, dp=dp, c_d=last_dcvt, all_h=all_h,
                          wdev=last_dev[dp]):
                        t.wait_ge(c_s.h, c_d)
                        t.wait_ge(hh.h, all_h)
                        if wdev > 0:
                            t.wait_ge(ev.h, wdev)
                        for k in range(SBLK):
                            mmi = t.matmul(out=ps_d[dp][:], lhsT=dbf[k][:, hdch * 128:(hdch + 1) * 128],
                                           rhs=hblk[k][:, tsub * 512:(tsub + 1) * 512],
                                           start=(k == 0), stop=(k == SBLK - 1))
                        mmi.then_inc(mm.h, 1)
                    c_dn = mm.inc()
                    @p.on("vector")
                    def _(v, hdch=hdch, tsub=tsub, dp=dp, c_dn=c_dn):
                        v.wait_ge(mm.h, c_dn)
                        sl = slice(tsub * 512, (tsub + 1) * 512)
                        v.tensor_tensor(out=dacc[hdch][:, sl], in0=dacc[hdch][:, sl],
                                        in1=ps_d[dp][:], op=ALU.add).then_inc(ev.h, 1)
                    last_dev[dp] = ev.inc()
            sb_dn_mark = mm.n

        for hdch in range(NDCH):
            gi = th * NDCH + hdch
            p.dma("sync", x1rd[:], x1d[hdch * 128:(hdch + 1) * 128, tok0:tok0 + HTOK],
                  dxr, waits=[(ev, ev.n)])
            vx = 16 * (gi + 1)
            @p.on("vector")
            def _(v, hdch=hdch, vx=vx, wev=ev.n, wst=sx2.n):
                v.wait_ge(ev.h, wev)
                v.wait_ge(dxr.h, vx)
                if wst > 0:
                    v.wait_ge(sx2.h, wst)
                v.tensor_tensor(out=x2st[:], in0=dacc[hdch][:], in1=x1rd[:],
                                op=ALU.add).then_inc(ev.h, 1)
            c_cmb = ev.inc()
            p.dma("gpsimd", x2T[hdch * 128:(hdch + 1) * 128, tok0:tok0 + HTOK],
                  x2st[:], sx2, waits=[(ev, c_cmb)])

    @p.on("gpsimd")
    def _(g, f1=sx1.n, f2=sx2.n, f3=ddbg.n):
        g.wait_ge(sx1.h, f1)
        g.wait_ge(sx2.h, f2)
        g.wait_ge(ddbg.h, f3)

    p.emit()
    es.close()
    return nc





def perm_24():
    return (4 * np.arange(2048)[None, :] + np.arange(4)[:, None]).reshape(8192)


def prep_D_core(x2_sl_T, pA_sl, pB_sl, inputs, wcache):
    if "gwD" not in wcache:
        perm = perm_24()
        wcache["gwD"] = tile_w_ffn_up(np.ascontiguousarray(inputs["L1_ffn_w_gate"][:, perm]))
        wcache["uwD"] = tile_w_ffn_up(np.ascontiguousarray(inputs["L1_ffn_w_up"][:, perm]))
        wcache["dwD"] = tile_w_ffn_down(np.ascontiguousarray(inputs["L1_ffn_w_down"][perm, :]))
    crow, ccol = _consts()
    return {
        "xT": x2_sl_T, "pA": pA_sl, "pB": pB_sl,
        "b_out": np.zeros((128, NDCH), np.float32),
        "gw": wcache["gwD"], "uw": wcache["uwD"], "dw": wcache["dwD"],
        "crow": crow, "ccol": ccol,
    }


def run_D(inputs, x2, branch):
    nc = build_D()
    in_maps = []
    wcache = {}
    for c in range(8):
        b, thf = c // 2, c % 2
        tsl = slice(thf * TOK, (thf + 1) * TOK)
        in_maps.append(prep_D_core(
            np.ascontiguousarray(x2[b, tsl].T),
            np.ascontiguousarray(branch[b, tsl].T),
            np.zeros((D, TOK), np.float32), inputs, wcache))
    res = run_bass_kernel_spmd(nc, in_maps, core_ids=list(range(8)))
    x4 = np.zeros((4, S, D), np.float32)
    for c in range(8):
        b, thf = c // 2, c % 2
        x4[b, thf * TOK:(thf + 1) * TOK] = res.results[c]["x4T"].T
    return x4



def build_A():
    nc = bass.Bass("TRN2", target_bir_lowering=False, debug=False)

    NT = S
    NH = 8
    xTf = nc.dram_tensor("xTf", [D, NT], F32, kind="ExternalInput").ap()
    wq = nc.dram_tensor("wq", [D, NH * 128], F32, kind="ExternalInput").ap()
    wk = nc.dram_tensor("wk", [D, NH * 128], F32, kind="ExternalInput").ap()
    wv = nc.dram_tensor("wv", [D, NH * 128], F32, kind="ExternalInput").ap()
    bqkv = nc.dram_tensor("bqkv", [128, 3 * NH], F32, kind="ExternalInput").ap()
    wo = nc.dram_tensor("wo", [NH * 128, D], F32, kind="ExternalInput").ap()
    masks = nc.dram_tensor("masks", [4 * 128, 512], BF16, kind="ExternalInput").ap()
    ident = nc.dram_tensor("ident", [128, 128], BF16, kind="ExternalInput").ap()
    crow = nc.dram_tensor("crow", [4, 128], F32, kind="ExternalInput").ap()
    ccol = nc.dram_tensor("ccol", [128, 4], F32, kind="ExternalInput").ap()

    partial = nc.dram_tensor("partial", [D, NT], F32, kind="ExternalOutput").ap()
    ln0d = nc.dram_tensor("ln0d", [D, NT], BF16, kind="Internal").ap()
    aod = nc.dram_tensor("aod", [NH * 128, NT], BF16, kind="Internal").ap()

    es = ExitStack()
    SB = lambda name, shape, dt: es.enter_context(nc.sbuf_tensor(name, shape, dt))
    PS = lambda name, shape, dt=F32: es.enter_context(nc.psum_tensor(name, shape, dt))

    ccols = SB("ccols", [128, 4], F32)
    ones_col = SB("ones_col", [128, 1], BF16)
    onesf_row = SB("onesf_row", [1, 128], F32)
    lnc_t = SB("lnc_t", [1, 1], F32)
    eps_t = SB("eps_t", [1, 1], F32)
    bcols = SB("bcols", [128, 3 * NH], F32)
    mask_t = [SB(f"mask{i}", [128, 512], BF16) for i in range(4)]
    ident_t = SB("ident_t", [128, 128], BF16)

    sems = mk_sems(es, nc, [
        "c_v", "c_s", "c_g", "mm", "ev", "hh", "ph",
        "dcst", "dx0", "dx1", "drs0", "drs1", "sln",
        "dw0", "dw1", "dsl0", "dsl1", "sao", "dao0", "dao1", "dwo", "spo"])
    c_v, c_s, c_g, mm, ev, hh, ph = (sems[k] for k in ["c_v", "c_s", "c_g", "mm", "ev", "hh", "ph"])
    dcst, sln, sao, dwo, spo = (sems[k] for k in ["dcst", "sln", "sao", "dwo", "spo"])
    dx = [sems["dx0"], sems["dx1"]]
    drs = [sems["drs0"], sems["drs1"]]
    dwt = [sems["dw0"], sems["dw1"]]
    dsl = [sems["dsl0"], sems["dsl1"]]
    dao = [sems["dao0"], sems["dao1"]]

    p = Plan(nc)

    p.dma("sync", ccols[:], ccol, dcst)
    p.dma("sync", onesf_row[:], crow[0:1, :], dcst)
    p.dma("sync", lnc_t[:], crow[2:3, 0:1], dcst)
    p.dma("sync", bcols[:], bqkv, dcst)
    for i in range(4):
        p.dma("sync", mask_t[i][:], masks[i * 128:(i + 1) * 128, :], dcst)
    v_const = p.dma("sync", ident_t[:], ident, dcst)

    @p.on("vector")
    def _(v):
        v.wait_ge(dcst.h, v_const)
        v.tensor_copy(out=ones_col[:], in_=ccols[:, 0:1])
        v.memset(eps_t[:], 1e-5)

    with ExitStack() as es1:
        SB1 = lambda name, shape, dt: es1.enter_context(nc.sbuf_tensor(name, shape, dt))
        PS1 = lambda name, shape: es1.enter_context(nc.psum_tensor(name, shape, F32))
        ldx = [SB1(f"ldx{i}", [128, NT], F32) for i in range(2)]
        s_sq = [SB1(f"s_sq{i}", [128, NT], BF16) for i in range(2)]
        s_c = [SB1(f"s_c{i}", [128, NT], BF16) for i in range(2)]
        mb_b = SB1("mb_b", [128, NT], F32)
        rb_b = SB1("rb_b", [128, NT], F32)
        sumx = SB1("sumx", [1, NT], F32)
        sumsq = SB1("sumsq", [1, NT], F32)
        scr_row = SB1("scr_row", [1, 512], F32)
        mrow, vrow = sumx, sumsq
        ps_sx = [PS1(f"ps_sx{i}", [1, 512]) for i in range(2)]
        ps_sq = [PS1(f"ps_sq{i}", [1, 512]) for i in range(2)]
        ps_bc = PS1("ps_bc", [128, 512])
        NSU = NT // 512

        @p.on("vector")
        def _(v):
            v.memset(sumx[:], 0.0)
            v.memset(sumsq[:], 0.0)

        cast_marks = []
        for dch in range(NDCH):
            sl2 = dch % 2
            lw = [(c_v, cast_marks[dch - 2])] if dch >= 2 else []
            vq = 16 * (dch // 2 + 1)
            p.dma("sync", ldx[sl2][:], xTf[dch * 128:(dch + 1) * 128, :], dx[sl2], waits=lw)
            @p.on("vector")
            def _(v, dch=dch, sl2=sl2, vq=vq, wmm=mm.n):
                v.wait_ge(dx[sl2].h, vq)
                if dch >= 2:
                    v.wait_ge(mm.h, wmm)
                v.tensor_copy(out=s_c[sl2][:], in_=ldx[sl2][:])
                v.tensor_tensor(out=s_sq[sl2][:], in0=s_c[sl2][:], in1=s_c[sl2][:],
                                op=ALU.mult).then_inc(c_v.h, 1)
            c_x = c_v.inc()
            cast_marks.append(c_x)
            for su in range(NSU):
                @p.on("tensor")
                def _(t, su=su, sl2=sl2, c_x=c_x, wev=ev.n):
                    t.wait_ge(c_v.h, c_x)
                    if wev > 1:
                        t.wait_ge(ev.h, wev - 1)
                    t.matmul(out=ps_sx[su % 2][:], lhsT=ones_col[:],
                             rhs=s_c[sl2][:, su * 512:(su + 1) * 512], start=True, stop=True)
                    t.matmul(out=ps_sq[su % 2][:], lhsT=ones_col[:],
                             rhs=s_sq[sl2][:, su * 512:(su + 1) * 512],
                             start=True, stop=True).then_inc(mm.h, 1)
                c_mm = mm.inc()
                @p.on("vector")
                def _(v, su=su, c_mm=c_mm):
                    v.wait_ge(mm.h, c_mm)
                    sl = slice(su * 512, (su + 1) * 512)
                    v.tensor_tensor(out=sumx[:, sl], in0=sumx[:, sl], in1=ps_sx[su % 2][:], op=ALU.add)
                    v.tensor_tensor(out=sumsq[:, sl], in0=sumsq[:, sl], in1=ps_sq[su % 2][:],
                                    op=ALU.add).then_inc(ev.h, 1)
                ev.inc()

        @p.on("vector")
        def _(v, wev=ev.n):
            v.wait_ge(ev.h, wev)
            v.tensor_scalar(out=mrow[:], in0=sumx[:], scalar1=1.0 / D, scalar2=None, op0=ALU.mult)
            v.tensor_scalar(out=vrow[:], in0=sumsq[:], scalar1=1.0 / D, scalar2=None, op0=ALU.mult)
            for su in range(NSU):
                sl = slice(su * 512, (su + 1) * 512)
                v.tensor_tensor(out=scr_row[:], in0=mrow[:, sl], in1=mrow[:, sl], op=ALU.mult)
                ti = v.tensor_tensor(out=vrow[:, sl], in0=vrow[:, sl], in1=scr_row[:], op=ALU.subtract)
            ti.then_inc(ev.h, 1)
        c_var = ev.inc()

        @p.on("scalar")
        def _(sc, c_var=c_var):
            sc.wait_ge(ev.h, c_var)
            sc.activation(out=vrow[:], in_=vrow[:], func=AF.Ln, bias=eps_t[:])
            sc.activation(out=vrow[:], in_=vrow[:], func=AF.Exp, scale=-0.5).then_inc(c_s.h, 1)
        c_rstd = c_s.inc()

        for (row, dst) in ((mrow, mb_b), (vrow, rb_b)):
            for su in range(NSU):
                @p.on("tensor")
                def _(t, row=row, su=su, c_rstd=c_rstd, wev=ev.n):
                    t.wait_ge(c_s.h, c_rstd)
                    t.wait_ge(ev.h, wev)
                    t.matmul(out=ps_bc[:], lhsT=onesf_row[:], rhs=row[:, su * 512:(su + 1) * 512],
                             start=True, stop=True).then_inc(mm.h, 1)
                c1 = mm.inc()
                @p.on("vector")
                def _(v, dst=dst, su=su, c1=c1):
                    v.wait_ge(mm.h, c1)
                    v.tensor_copy(out=dst[:, su * 512:(su + 1) * 512], in_=ps_bc[:]).then_inc(ev.h, 1)
                ev.inc()

        c_bcast = ev.n
        norm_marks = []
        for dch in range(NDCH):
            sl2 = dch % 2
            waits = [(ev, c_bcast if dch < 2 else norm_marks[dch - 2])]
            vx = 16 * (dch // 2 + 1)
            p.dma("sync", ldx[sl2][:], xTf[dch * 128:(dch + 1) * 128, :], drs[sl2], waits=waits)
            @p.on("vector")
            def _(v, dch=dch, sl2=sl2, vx=vx, c_bcast=c_bcast, wsl=16 * dch):
                v.wait_ge(drs[sl2].h, vx)
                v.wait_ge(ev.h, c_bcast)
                if wsl >= 32:
                    v.wait_ge(sln.h, wsl - 16)
                v.tensor_tensor(out=ldx[sl2][:], in0=ldx[sl2][:], in1=mb_b[:], op=ALU.subtract)
                v.tensor_tensor(out=ldx[sl2][:], in0=ldx[sl2][:], in1=rb_b[:], op=ALU.mult)
                v.tensor_copy(out=s_c[sl2][:], in_=ldx[sl2][:]).then_inc(ev.h, 1)
            norm_marks.append(ev.inc())
            p.dma("gpsimd", ln0d[dch * 128:(dch + 1) * 128, :], s_c[sl2][:], sln,
                  waits=[(ev, norm_marks[-1])])

        @p.on("vector")
        def _(v):
            v.tensor_copy(out=scr_row[0:1, 0:1], in_=ldx[1][0:1, 0:1]).then_inc(ph.h, 1)
        ph.inc()
    ln_stores = sln.n

    q_h = SB("q_h", [128, NT], BF16)
    k_h = SB("k_h", [128, NT], BF16)
    v_h = SB("v_h", [128, NT], BF16)
    vT = [SB(f"vT{i}", [128, 128], BF16) for i in range(32)]
    lnsl = [SB(f"lnsl{i}", [128, 512 * NDCH], BF16) for i in range(2)]
    wti_f = SB("wti_f", [128, 3 * 128 * NDCH], F32)
    wtb_all = SB("wtb_all", [128, 3 * 128 * NDCH], BF16)
    s_k2 = SB("s_k2", [128, 512], BF16)
    mx_t = SB("mx_t", [1, 16], F32)
    mk_t = SB("mk_t", [1, 1], F32)
    mq_t = SB("mq_t", [1, 1], F32)
    c1_t = SB("c1_t", [1, 1], F32)
    c2_t = SB("c2_t", [1, 1], F32)
    negc = SB("negc", [128, 1], F32)
    PT = [SB(f"PT{i}", [128, 512], BF16) for i in range(2)]
    o_ev = SB("o_ev", [128, 512], F32)
    r_ev = SB("r_ev", [128, 512], F32)
    rrow = SB("rrow", [1, 512], F32)
    ao_h = SB("ao_h", [128, NT], BF16)

    ps_s = [PS(f"ps_s{i}", [128, 512]) for i in range(2)]
    ps_o = PS("ps_o", [128, 512])
    ps_dn = PS("ps_dn", [1, 512])
    ps_r = PS("ps_r", [128, 512])
    ps_t = PS("ps_t", [128, 128], BF16)
    ps_q, ps_k, ps_v = ps_s[0], ps_s[1], ps_o

    for eng in ("tensor", "scalar", "gpsimd", "sync"):
        @p.on(eng)
        def _(e):
            e.wait_ge(ph.h, 1)

    last_sl_use = [0, 0]
    last_wcvt = 0
    prev_c_c2 = 0
    for h in range(NH):
        lw2 = [(c_v, last_wcvt)] if last_wcvt > 0 else []
        for dch in range(NDCH):
            p.dma("gpsimd", wti_f[:, dch * 384:dch * 384 + 128],
                  wq[dch * 128:(dch + 1) * 128, h * 128:(h + 1) * 128], dwt[0],
                  waits=lw2 if dch == 0 else [])
            p.dma("gpsimd", wti_f[:, dch * 384 + 128:dch * 384 + 256],
                  wk[dch * 128:(dch + 1) * 128, h * 128:(h + 1) * 128], dwt[0])
            p.dma("gpsimd", wti_f[:, dch * 384 + 256:dch * 384 + 384],
                  wv[dch * 128:(dch + 1) * 128, h * 128:(h + 1) * 128], dwt[0])
        v_w = dwt[0].n
        @p.on("vector")
        def _(v, v_w=v_w, wmm=mm.n):
            v.wait_ge(dwt[0].h, v_w)
            if wmm > 0:
                v.wait_ge(mm.h, wmm)
            v.tensor_copy(out=wtb_all[:], in_=wti_f[:]).then_inc(c_v.h, 1)
        last_wcvt = c_v.inc()

        for tsub in range(NSU):
            slw = tsub % 2
            lw = [(mm, last_sl_use[slw])] if last_sl_use[slw] > 0 else [(sln, ln_stores)]
            for dch in range(NDCH):
                p.dma("sync" if dch % 2 == 0 else "scalar",
                      lnsl[slw][:, dch * 512:(dch + 1) * 512],
                      ln0d[dch * 128:(dch + 1) * 128, tsub * 512:(tsub + 1) * 512],
                      dsl[slw], waits=lw if dch <= 1 else [])
            v_sl = dsl[slw].n
            @p.on("tensor")
            def _(t, slw=slw, v_sl=v_sl, cw=last_wcvt, wev=ev.n):
                t.wait_ge(c_v.h, cw)
                t.wait_ge(dsl[slw].h, v_sl)
                t.wait_ge(ev.h, wev)
                for dch in range(NDCH):
                    ls = slice(dch * 512, dch * 512 + 512)
                    t.matmul(out=ps_q[:], lhsT=wtb_all[:, dch * 384:dch * 384 + 128],
                             rhs=lnsl[slw][:, ls],
                             start=(dch == 0), stop=(dch == NDCH - 1))
                for dch in range(NDCH):
                    ls = slice(dch * 512, dch * 512 + 512)
                    t.matmul(out=ps_k[:], lhsT=wtb_all[:, dch * 384 + 128:dch * 384 + 256],
                             rhs=lnsl[slw][:, ls],
                             start=(dch == 0), stop=(dch == NDCH - 1))
                for dch in range(NDCH):
                    ls = slice(dch * 512, dch * 512 + 512)
                    mmi = t.matmul(out=ps_v[:], lhsT=wtb_all[:, dch * 384 + 256:dch * 384 + 384],
                                   rhs=lnsl[slw][:, ls],
                                   start=(dch == 0), stop=(dch == NDCH - 1))
                mmi.then_inc(mm.h, 1)
            c_pr = mm.inc()
            last_sl_use[slw] = c_pr
            @p.on("vector")
            def _(v, h=h, tsub=tsub, c_pr=c_pr):
                v.wait_ge(mm.h, c_pr)
                sl = slice(tsub * 512, (tsub + 1) * 512)
                v.tensor_scalar(out=q_h[:, sl], in0=ps_q[:], scalar1=bcols[:, 3 * h:3 * h + 1],
                                scalar2=None, op0=ALU.add)
                v.tensor_scalar(out=k_h[:, sl], in0=ps_k[:], scalar1=bcols[:, 3 * h + 1:3 * h + 2],
                                scalar2=None, op0=ALU.add)
                v.tensor_scalar(out=v_h[:, sl], in0=ps_v[:], scalar1=bcols[:, 3 * h + 2:3 * h + 3],
                                scalar2=None, op0=ALU.add).then_inc(ev.h, 1)
            ev.inc()
        c_qkv = ev.n

        for kc in range(32):
            @p.on("tensor")
            def _(t, kc=kc, c_qkv=c_qkv, wev=ev.n):
                t.wait_ge(ev.h, c_qkv)
                if kc >= 1:
                    t.wait_ge(ev.h, wev)
                t.transpose(out=ps_t[:], in_=v_h[:, kc * 128:(kc + 1) * 128],
                            identity=ident_t[:]).then_inc(mm.h, 1)
            c_t = mm.inc()
            @p.on("vector")
            def _(v, kc=kc, c_t=c_t):
                v.wait_ge(mm.h, c_t)
                v.tensor_copy(out=vT[kc][:], in_=ps_t[:]).then_inc(ev.h, 1)
            ev.inc()

        for (pi, buf) in ((0, k_h), (1, q_h)):
            for su in range(NSU):
                @p.on("vector")
                def _(v, buf=buf, su=su, wmm=mm.n, c_qkv=c_qkv):
                    v.wait_ge(ev.h, c_qkv)
                    if wmm > 0:
                        v.wait_ge(mm.h, wmm)
                    v.tensor_tensor(out=s_k2[:], in0=buf[:, su * 512:(su + 1) * 512],
                                    in1=buf[:, su * 512:(su + 1) * 512], op=ALU.mult).then_inc(c_v.h, 1)
                c_k2 = c_v.inc()
                @p.on("tensor")
                def _(t, c_k2=c_k2, wev=ev.n):
                    t.wait_ge(c_v.h, c_k2)
                    if wev > 0:
                        t.wait_ge(ev.h, wev)
                    t.matmul(out=ps_dn[:], lhsT=ones_col[:], rhs=s_k2[:],
                             start=True, stop=True).then_inc(mm.h, 1)
                c_nm = mm.inc()
                @p.on("vector")
                def _(v, pi=pi, su=su, c_nm=c_nm):
                    v.wait_ge(mm.h, c_nm)
                    v.reduce_max(out=mx_t[:, pi * 8 + su:pi * 8 + su + 1],
                                 in_=ps_dn[:], axis=AX.X).then_inc(ev.h, 1)
                ev.inc()
        @p.on("vector")
        def _(v, wev=ev.n):
            v.wait_ge(ev.h, wev)
            v.reduce_max(out=mk_t[:], in_=mx_t[:, 0:8], axis=AX.X)
            v.reduce_max(out=mq_t[:], in_=mx_t[:, 8:16], axis=AX.X).then_inc(c_v.h, 1)
        c_mx = c_v.inc()
        @p.on("vector")
        def _(v, c_mx=c_mx):
            v.wait_ge(c_v.h, c_mx)
            v.tensor_tensor(out=c1_t[:], in0=mk_t[:], in1=mq_t[:], op=ALU.mult).then_inc(c_v.h, 1)
        c_c1 = c_v.inc()
        @p.on("scalar")
        def _(sc, c_c1=c_c1):
            sc.wait_ge(c_v.h, c_c1)
            sc.activation(out=c2_t[:], in_=c1_t[:], func=AF.Ln).then_inc(c_s.h, 1)
        c_ln = c_s.inc()
        @p.on("scalar")
        def _(sc, c_ln=c_ln):
            sc.wait_ge(c_s.h, c_ln)
            sc.activation(out=c1_t[:], in_=c2_t[:], func=AF.Exp, scale=0.5,
                          bias=lnc_t[:]).then_inc(c_s.h, 1)
        c_c2 = c_s.inc()
        prev_c_c2 = c_c2
        @p.on("vector")
        def _(v, c_c2=c_c2):
            v.wait_ge(c_s.h, c_c2)
            v.tensor_scalar(out=c2_t[:], in0=c1_t[:], scalar1=-1.0, scalar2=None,
                            op0=ALU.mult).then_inc(c_v.h, 1)
        c_c3 = c_v.inc()
        @p.on("tensor")
        def _(t, c_c3=c_c3):
            t.wait_ge(c_v.h, c_c3)
            t.matmul(out=ps_r[:, 0:1], lhsT=onesf_row[:], rhs=c2_t[:],
                     start=True, stop=True).then_inc(mm.h, 1)
        c_c4 = mm.inc()
        @p.on("vector")
        def _(v, c_c4=c_c4):
            v.wait_ge(mm.h, c_c4)
            v.tensor_copy(out=negc[:], in_=ps_r[:, 0:1]).then_inc(c_v.h, 1)
        c_negc = c_v.inc()

        SC = 0.08838834764831845
        last_pt = [0, 0]
        prev_oe = 0
        prev_rc = 0
        for qc in range(NSU):
            nkj = 4 * qc + 4
            for kj in range(nkj):
                sp = kj % 2
                @p.on("tensor")
                def _(t, h=h, qc=qc, kj=kj, sp=sp, c_negc=c_negc, wpt=last_pt[sp]):
                    t.wait_ge(c_v.h, c_negc)
                    if wpt > 0:
                        t.wait_ge(c_s.h, wpt)
                    t.matmul(out=ps_s[sp][:], lhsT=k_h[:, kj * 128:(kj + 1) * 128],
                             rhs=q_h[:, qc * 512:(qc + 1) * 512],
                             start=True, stop=True).then_inc(mm.h, 1)
                c_sc = mm.inc()
                @p.on("scalar")
                def _(sc_, kj=kj, sp=sp, c_sc=c_sc, wv2=hh.n):
                    sc_.wait_ge(mm.h, c_sc)
                    if wv2 > 1:
                        sc_.wait_ge(hh.h, wv2 - 1)
                    sc_.activation(out=PT[sp][:], in_=ps_s[sp][:], func=AF.Exp,
                                   bias=negc[:], scale=SC).then_inc(c_s.h, 1)
                last_pt[sp] = c_s.inc()
                c_e = last_pt[sp]
                s_off = kj - 4 * qc
                if s_off >= 0:
                    @p.on("vector")
                    def _(v, sp=sp, s_off=s_off, c_e=c_e):
                        v.wait_ge(c_s.h, c_e)
                        v.tensor_tensor(out=PT[sp][:], in0=PT[sp][:], in1=mask_t[s_off][:],
                                        op=ALU.mult).then_inc(c_v.h, 1)
                    c_e = (c_v.inc(), "v")
                @p.on("tensor")
                def _(t, kj=kj, sp=sp, c_e=c_e, prev_oe=prev_oe, prev_rc=prev_rc):
                    if isinstance(c_e, tuple):
                        t.wait_ge(c_v.h, c_e[0])
                    else:
                        t.wait_ge(c_s.h, c_e)
                    if kj == 0:
                        if prev_oe > 0:
                            t.wait_ge(c_s.h, prev_oe)
                        if prev_rc > 0:
                            t.wait_ge(c_v.h, prev_rc)
                    t.matmul(out=ps_o[:], lhsT=vT[kj][:], rhs=PT[sp][:],
                             start=(kj == 0), stop=(kj == nkj - 1))
                    t.matmul(out=ps_dn[:], lhsT=ones_col[:], rhs=PT[sp][:],
                             start=(kj == 0), stop=(kj == nkj - 1)).then_inc(hh.h, 1)
                hh.inc()
            c_av = hh.n
            @p.on("vector")
            def _(v, c_av=c_av):
                v.wait_ge(hh.h, c_av)
                v.reciprocal(out=rrow[:], in_=ps_dn[:]).then_inc(c_v.h, 1)
            c_rc = c_v.inc()
            prev_rc = c_rc
            @p.on("tensor")
            def _(t, c_rc=c_rc):
                t.wait_ge(c_v.h, c_rc)
                t.matmul(out=ps_r[:], lhsT=onesf_row[:], rhs=rrow[:],
                         start=True, stop=True).then_inc(mm.h, 1)
            c_rb = mm.inc()
            @p.on("scalar")
            def _(sc, c_rb=c_rb, wev=ev.n):
                sc.wait_ge(mm.h, c_rb)
                if wev > 0:
                    sc.wait_ge(ev.h, wev)
                sc.activation(out=o_ev[:], in_=ps_o[:], func=AF.Copy).then_inc(c_s.h, 1)
            c_oe = c_s.inc()
            prev_oe = c_oe
            @p.on("vector")
            def _(v, qc=qc, c_oe=c_oe, c_rb=c_rb, wsao=16 * h):
                v.wait_ge(c_s.h, c_oe)
                v.wait_ge(mm.h, c_rb)
                if qc == 0 and wsao > 0:
                    v.wait_ge(sao.h, wsao)
                v.tensor_copy(out=r_ev[:], in_=ps_r[:])
                v.tensor_tensor(out=ao_h[:, qc * 512:(qc + 1) * 512], in0=o_ev[:], in1=r_ev[:],
                                op=ALU.mult).then_inc(ev.h, 1)
            ev.inc()
        p.dma("gpsimd", aod[h * 128:(h + 1) * 128, :], ao_h[:], sao, waits=[(ev, ev.n)])

    wo_f = SB("wo_f", [128, D], F32)
    wo_b = [SB(f"wo_b{i}", [128, D], BF16) for i in range(NH)]
    aosl = [SB(f"aosl{i}", [128, 512 * NH], BF16) for i in range(2)]
    po_ev = [SB(f"po_ev{i}", [128, 512], F32) for i in range(2)]
    ps_po = [PS(f"ps_po{i}", [128, 512]) for i in range(2)]

    for dh in range(NH):
        vw = p.dma("sync", wo_f[:], wo[dh * 128:(dh + 1) * 128, :], dwo,
                   waits=[(c_v, c_v.n) if dh > 0 else (ph, 1)])
        @p.on("vector")
        def _(v, dh=dh, vw=vw):
            v.wait_ge(dwo.h, vw)
            v.tensor_copy(out=wo_b[dh][:], in_=wo_f[:]).then_inc(c_v.h, 1)
        c_v.inc()
    c_wo = c_v.n
    all_ao = sao.n

    for tsub in range(NSU):
        slw = tsub % 2
        lw = [(mm, last_sl_use[slw])] if tsub >= 2 else [(sao, all_ao)]
        for dh in range(NH):
            p.dma("sync" if dh % 2 == 0 else "scalar",
                  aosl[slw][:, dh * 512:(dh + 1) * 512],
                  aod[dh * 128:(dh + 1) * 128, tsub * 512:(tsub + 1) * 512],
                  dao[slw], waits=lw if dh <= 1 else [])
        v_sl = dao[slw].n
        for ech in range(NDCH):
            pp = ech % 2
            @p.on("tensor")
            def _(t, tsub=tsub, ech=ech, pp=pp, slw=slw, v_sl=v_sl, c_wo=c_wo, wev=ev.n):
                t.wait_ge(c_v.h, c_wo)
                t.wait_ge(dao[slw].h, v_sl)
                if wev > 0:
                    t.wait_ge(ev.h, wev - 1 if (ech > 1 or tsub > 0) else 0)
                for dh in range(NH):
                    mmi = t.matmul(out=ps_po[pp][:], lhsT=wo_b[dh][:, ech * 128:(ech + 1) * 128],
                                   rhs=aosl[slw][:, dh * 512:(dh + 1) * 512],
                                   start=(dh == 0), stop=(dh == NH - 1))
                mmi.then_inc(mm.h, 1)
            c_po = mm.inc()
            last_sl_use[slw] = c_po
            gi6 = tsub * NDCH + ech
            @p.on("vector")
            def _(v, tsub=tsub, ech=ech, pp=pp, c_po=c_po, gi6=gi6):
                v.wait_ge(mm.h, c_po)
                if gi6 >= 2:
                    v.wait_ge(spo.h, 16 * (gi6 - 1))
                v.tensor_copy(out=po_ev[gi6 % 2][:], in_=ps_po[pp][:]).then_inc(ev.h, 1)
            c_pe = ev.inc()
            p.dma("gpsimd", partial[ech * 128:(ech + 1) * 128, tsub * 512:(tsub + 1) * 512],
                  po_ev[gi6 % 2][:], spo, waits=[(ev, c_pe)])

    @p.on("gpsimd")
    def _(g, f1=spo.n, f2=sao.n, f3=sln.n):
        g.wait_ge(spo.h, f1)
        g.wait_ge(sao.h, f2)
        g.wait_ge(sln.h, f3)

    p.emit()
    es.close()
    return nc


def prep_A_core(x_b_T, hh, inputs):
    wqkv = inputs["L0_attn_w_qkv"]
    b = inputs["L0_attn_b_qkv"]
    NH = 8
    q0 = hh * 1024
    wq = np.ascontiguousarray(wqkv[q0:q0 + 1024, :].T)
    wk = np.ascontiguousarray(wqkv[2048 + q0:2048 + q0 + 1024, :].T)
    wv = np.ascontiguousarray(wqkv[4096 + q0:4096 + q0 + 1024, :].T)
    bq = b[q0:q0 + 1024].reshape(8, 128)
    bk = b[2048 + q0:2048 + q0 + 1024].reshape(8, 128)
    bv = b[4096 + q0:4096 + q0 + 1024].reshape(8, 128)
    bcols = np.zeros((128, 24), np.float32)
    for h in range(8):
        bcols[:, 3 * h] = bq[h]
        bcols[:, 3 * h + 1] = bk[h]
        bcols[:, 3 * h + 2] = bv[h]
    wo = np.ascontiguousarray(inputs["L0_attn_w_out"][:, hh * 1024:(hh + 1) * 1024].T)
    import ml_dtypes
    masks = np.zeros((4, 128, 512), np.float32)
    jj = np.arange(128)[:, None]
    ii = np.arange(512)[None, :]
    for s in range(4):
        masks[s] = (jj <= ii - 128 * s).astype(np.float32)
    crow, ccol = _consts()
    crow[2, :] = np.log(1.0 / np.sqrt(128.0))
    return {
        "xTf": x_b_T, "wq": wq, "wk": wk, "wv": wv, "bqkv": bcols, "wo": wo,
        "masks": masks.reshape(512, 512).astype(ml_dtypes.bfloat16),
        "ident": np.eye(128, dtype=np.float32).astype(ml_dtypes.bfloat16),
        "crow": crow, "ccol": ccol,
    }


def run_A(inputs, x):
    nc = build_A()
    in_maps = []
    for c in range(8):
        b, hh = c // 2, c % 2
        in_maps.append(prep_A_core(np.ascontiguousarray(x[b].T), hh, inputs))
    res = run_bass_kernel_spmd(nc, in_maps, core_ids=list(range(8)))
    out = np.zeros((4, S, D), np.float32)
    for c in range(8):
        b = c // 2
        out[b] += res.results[c]["partial"].T
    return out



def build_C():
    nc = bass.Bass("TRN2", target_bir_lowering=False, debug=False)

    NT = S
    NHV = 16
    NKQ = 8
    NCH = 32
    x2Tf = nc.dram_tensor("x2Tf", [D, NT], F32, kind="ExternalInput").ap()
    wqkvT = nc.dram_tensor("wqkvT", [D, 4096], F32, kind="ExternalInput").ap()
    wzT = nc.dram_tensor("wzT", [D, 2048], F32, kind="ExternalInput").ap()
    wabT = nc.dram_tensor("wabT", [D, 32], F32, kind="ExternalInput").ap()
    convw = nc.dram_tensor("convw", [4096, 4], F32, kind="ExternalInput").ap()
    dtb = nc.dram_tensor("dtb", [16, 1], F32, kind="ExternalInput").ap()
    alog = nc.dram_tensor("alog", [16, 1], F32, kind="ExternalInput").ap()
    nweight = nc.dram_tensor("nweight", [128, 1], F32, kind="ExternalInput").ap()
    woT = nc.dram_tensor("woT", [2048, D], F32, kind="ExternalInput").ap()
    masklow = nc.dram_tensor("masklow", [128, 128], F32, kind="ExternalInput").ap()
    maskup = nc.dram_tensor("maskup", [128, 128], F32, kind="ExternalInput").ap()
    ident = nc.dram_tensor("identC", [128, 128], BF16, kind="ExternalInput").ap()
    identf = nc.dram_tensor("identf", [128, 128], F32, kind="ExternalInput").ap()
    crow = nc.dram_tensor("crow", [4, 128], F32, kind="ExternalInput").ap()
    ccol = nc.dram_tensor("ccol", [128, 4], F32, kind="ExternalInput").ap()

    partial = nc.dram_tensor("partial", [D, NT], F32, kind="ExternalOutput").ap()
    ln2d = nc.dram_tensor("ln2d", [D, NT], BF16, kind="Internal").ap()
    qcd = nc.dram_tensor("qcd", [4096, NT], BF16, kind="Internal").ap()
    qpd = nc.dram_tensor("qpd", [4096, NT], BF16, kind="Internal").ap()
    zd = nc.dram_tensor("zd", [2048, NT], BF16, kind="Internal").ap()
    abd = nc.dram_tensor("abd", [32, NT], F32, kind="Internal").ap()
    gcd = nc.dram_tensor("gcd", [16, NT], F32, kind="Internal").ap()
    btd = nc.dram_tensor("btd", [16, NT], F32, kind="Internal").ap()
    gtd = nc.dram_tensor("gtd", [NT, 16], F32, kind="Internal").ap()
    cod = nc.dram_tensor("cod", [2048, NT], F32, kind="Internal").ap()
    gtd2 = nc.dram_tensor("gtd2", [2048, NT], BF16, kind="Internal").ap()

    es = ExitStack()
    SB = lambda name, shape, dt: es.enter_context(nc.sbuf_tensor(name, shape, dt))

    ccols = SB("ccols", [128, 4], F32)
    ones_col = SB("ones_col", [128, 1], BF16)
    onesf_row = SB("onesf_row", [1, 128], F32)
    ones_t = SB("ones_t", [128, 128], F32)
    onesf_col = SB("onesf_col", [128, 1], F32)
    lnc_t = SB("lnc_t", [1, 1], F32)
    eps5 = SB("eps5", [1, 1], F32)
    eps6 = SB("eps6", [1, 1], F32)
    one1 = SB("one1", [16, 1], F32)
    dtb_t = SB("dtb_t", [16, 1], F32)
    nec_t = SB("nec_t", [16, 1], F32)
    nw_t = SB("nw_t", [128, 1], F32)
    ml_t = SB("ml_t", [128, 128], F32)
    mu_t = SB("mu_t", [128, 128], F32)
    id_bf = SB("id_bf", [128, 128], BF16)
    id_f = SB("id_f", [128, 128], F32)
    id4_bf = SB("id4_bf", [128, 512], BF16)

    sems = mk_sems(es, nc, [
        "c_v", "c_s", "c_g", "mm", "ev", "hh", "ph", "ph2", "ph3",
        "dcst", "dx0", "dx1", "drs0", "drs1", "sln",
        "dw0", "dw1", "dsl0", "dsl1", "sq0", "sq1", "dq0", "dq1", "scv",
        "dgr", "dv0", "dv1", "sco", "dco0", "dco1", "dz0", "dz1", "sgt",
        "dgs0", "dgs1", "dwo", "spo", "dgl", "dwab", "sab", "dab"])
    c_v, c_s, c_g, mm, ev, hh, ph, ph2, ph3 = (sems[k] for k in
        ["c_v", "c_s", "c_g", "mm", "ev", "hh", "ph", "ph2", "ph3"])
    dcst, sln, scv, dgr, sco, sgt, dwo, spo, dgl, dwab, sab, dab = (sems[k] for k in
        ["dcst", "sln", "scv", "dgr", "sco", "sgt", "dwo", "spo", "dgl", "dwab", "sab", "dab"])
    dx = [sems["dx0"], sems["dx1"]]
    drs = [sems["drs0"], sems["drs1"]]
    dwt = [sems["dw0"], sems["dw1"]]
    dsl = [sems["dsl0"], sems["dsl1"]]
    sq = [sems["sq0"], sems["sq1"]]
    dq = [sems["dq0"], sems["dq1"]]
    dv_ = [sems["dv0"], sems["dv1"]]
    dco = [sems["dco0"], sems["dco1"]]
    dz = [sems["dz0"], sems["dz1"]]
    dgs = [sems["dgs0"], sems["dgs1"]]

    p = Plan(nc)

    p.dma("sync", ccols[:], ccol, dcst)
    p.dma("sync", onesf_row[:], crow[0:1, :], dcst)
    p.dma("sync", lnc_t[:], crow[2:3, 0:1], dcst)
    p.dma("sync", dtb_t[:], dtb, dcst)
    p.dma("sync", nec_t[:], alog, dcst)
    p.dma("sync", nw_t[:], nweight, dcst)
    p.dma("sync", ml_t[:], masklow, dcst)
    p.dma("sync", mu_t[:], maskup, dcst)
    p.dma("sync", id_bf[:], ident, dcst)
    v_const = p.dma("sync", id_f[:], identf, dcst)

    @p.on("vector")
    def _(v):
        v.wait_ge(dcst.h, v_const)
        v.tensor_copy(out=ones_col[:], in_=ccols[:, 0:1])
        v.tensor_copy(out=onesf_col[:], in_=ccols[:, 0:1])
        v.memset(ones_t[:], 1.0)
        v.memset(eps5[:], 1e-5)
        v.memset(eps6[:], 1e-6)
        v.memset(one1[:], 1.0)
        for r in range(4):
            v.tensor_copy(out=id4_bf[:, r * 128:(r + 1) * 128], in_=id_bf[:])
    @p.on("scalar")
    def _(sc):
        sc.wait_ge(dcst.h, v_const)
        sc.activation(out=nec_t[:], in_=nec_t[:], func=AF.Exp).then_inc(c_s.h, 1)
    c_nec0 = c_s.inc()
    @p.on("vector")
    def _(v, c=c_nec0):
        v.wait_ge(c_s.h, c)
        v.tensor_scalar(out=nec_t[:], in0=nec_t[:], scalar1=-1.0, scalar2=None,
                        op0=ALU.mult).then_inc(c_v.h, 1)
    c_nec = c_v.inc()

    with ExitStack() as es1:
        SB1 = lambda name, shape, dt: es1.enter_context(nc.sbuf_tensor(name, shape, dt))
        PS1 = lambda name, shape: es1.enter_context(nc.psum_tensor(name, shape, F32))
        ldx = [SB1(f"ldx{i}", [128, NT], F32) for i in range(2)]
        s_sq = [SB1(f"s_sq{i}", [128, NT], BF16) for i in range(2)]
        s_c = [SB1(f"s_c{i}", [128, NT], BF16) for i in range(2)]
        mb_b = SB1("mb_b", [128, NT], F32)
        rb_b = SB1("rb_b", [128, NT], F32)
        sumx = SB1("sumx", [1, NT], F32)
        sumsq = SB1("sumsq", [1, NT], F32)
        scr_row = SB1("scr_row", [1, 512], F32)
        mrow, vrow = sumx, sumsq
        ps_sx = [PS1(f"ps_sx{i}", [1, 512]) for i in range(2)]
        ps_sq = [PS1(f"ps_sq{i}", [1, 512]) for i in range(2)]
        ps_bc = PS1("ps_bc", [128, 512])
        NSU = NT // 512

        @p.on("vector")
        def _(v):
            v.memset(sumx[:], 0.0)
            v.memset(sumsq[:], 0.0)

        cast_marks = []
        for dch in range(NDCH):
            sl2 = dch % 2
            lw = [(c_v, cast_marks[dch - 2])] if dch >= 2 else []
            vq = 16 * (dch // 2 + 1)
            p.dma("sync", ldx[sl2][:], x2Tf[dch * 128:(dch + 1) * 128, :], dx[sl2], waits=lw)
            @p.on("vector")
            def _(v, sl2=sl2, vq=vq, dch=dch, wmm=mm.n):
                v.wait_ge(dx[sl2].h, vq)
                if dch >= 2:
                    v.wait_ge(mm.h, wmm)
                v.tensor_copy(out=s_c[sl2][:], in_=ldx[sl2][:])
                v.tensor_tensor(out=s_sq[sl2][:], in0=s_c[sl2][:], in1=s_c[sl2][:],
                                op=ALU.mult).then_inc(c_v.h, 1)
            c_x = c_v.inc()
            cast_marks.append(c_x)
            for su in range(NSU):
                @p.on("tensor")
                def _(t, su=su, sl2=sl2, c_x=c_x, wev=ev.n):
                    t.wait_ge(c_v.h, c_x)
                    if wev > 1:
                        t.wait_ge(ev.h, wev - 1)
                    t.matmul(out=ps_sx[su % 2][:], lhsT=ones_col[:],
                             rhs=s_c[sl2][:, su * 512:(su + 1) * 512], start=True, stop=True)
                    t.matmul(out=ps_sq[su % 2][:], lhsT=ones_col[:],
                             rhs=s_sq[sl2][:, su * 512:(su + 1) * 512],
                             start=True, stop=True).then_inc(mm.h, 1)
                c_mm = mm.inc()
                @p.on("vector")
                def _(v, su=su, c_mm=c_mm):
                    v.wait_ge(mm.h, c_mm)
                    sl = slice(su * 512, (su + 1) * 512)
                    v.tensor_tensor(out=sumx[:, sl], in0=sumx[:, sl], in1=ps_sx[su % 2][:], op=ALU.add)
                    v.tensor_tensor(out=sumsq[:, sl], in0=sumsq[:, sl], in1=ps_sq[su % 2][:],
                                    op=ALU.add).then_inc(ev.h, 1)
                ev.inc()

        @p.on("vector")
        def _(v, wev=ev.n):
            v.wait_ge(ev.h, wev)
            v.tensor_scalar(out=mrow[:], in0=sumx[:], scalar1=1.0 / D, scalar2=None, op0=ALU.mult)
            v.tensor_scalar(out=vrow[:], in0=sumsq[:], scalar1=1.0 / D, scalar2=None, op0=ALU.mult)
            for su in range(NSU):
                sl = slice(su * 512, (su + 1) * 512)
                v.tensor_tensor(out=scr_row[:], in0=mrow[:, sl], in1=mrow[:, sl], op=ALU.mult)
                ti = v.tensor_tensor(out=vrow[:, sl], in0=vrow[:, sl], in1=scr_row[:], op=ALU.subtract)
            ti.then_inc(ev.h, 1)
        c_var = ev.inc()

        @p.on("scalar")
        def _(sc, c_var=c_var):
            sc.wait_ge(ev.h, c_var)
            sc.activation(out=vrow[:], in_=vrow[:], func=AF.Ln, bias=eps5[:])
            sc.activation(out=vrow[:], in_=vrow[:], func=AF.Exp, scale=-0.5).then_inc(c_s.h, 1)
        c_rstd = c_s.inc()

        for (row, dst) in ((mrow, mb_b), (vrow, rb_b)):
            for su in range(NSU):
                @p.on("tensor")
                def _(t, row=row, su=su, c_rstd=c_rstd, wev=ev.n):
                    t.wait_ge(c_s.h, c_rstd)
                    t.wait_ge(ev.h, wev)
                    t.matmul(out=ps_bc[:], lhsT=onesf_row[:], rhs=row[:, su * 512:(su + 1) * 512],
                             start=True, stop=True).then_inc(mm.h, 1)
                c1 = mm.inc()
                @p.on("vector")
                def _(v, dst=dst, su=su, c1=c1):
                    v.wait_ge(mm.h, c1)
                    v.tensor_copy(out=dst[:, su * 512:(su + 1) * 512], in_=ps_bc[:]).then_inc(ev.h, 1)
                ev.inc()

        c_bcast = ev.n
        norm_marks = []
        for dch in range(NDCH):
            sl2 = dch % 2
            waits = [(ev, c_bcast if dch < 2 else norm_marks[dch - 2])]
            vx = 16 * (dch // 2 + 1)
            p.dma("sync", ldx[sl2][:], x2Tf[dch * 128:(dch + 1) * 128, :], drs[sl2], waits=waits)
            @p.on("vector")
            def _(v, dch=dch, sl2=sl2, vx=vx, c_bcast=c_bcast, wsl=16 * dch):
                v.wait_ge(drs[sl2].h, vx)
                v.wait_ge(ev.h, c_bcast)
                if wsl >= 32:
                    v.wait_ge(sln.h, wsl - 16)
                v.tensor_tensor(out=ldx[sl2][:], in0=ldx[sl2][:], in1=mb_b[:], op=ALU.subtract)
                v.tensor_tensor(out=ldx[sl2][:], in0=ldx[sl2][:], in1=rb_b[:], op=ALU.mult)
                v.tensor_copy(out=s_c[sl2][:], in_=ldx[sl2][:]).then_inc(ev.h, 1)
            norm_marks.append(ev.inc())
            p.dma("gpsimd", ln2d[dch * 128:(dch + 1) * 128, :], s_c[sl2][:], sln,
                  waits=[(ev, norm_marks[-1])])

        @p.on("vector")
        def _(v):
            v.tensor_copy(out=scr_row[0:1, 0:1], in_=ldx[1][0:1, 0:1]).then_inc(ph.h, 1)
        ph.inc()
    ln_stores = sln.n

    PS = lambda name, shape, dt=F32: es.enter_context(nc.psum_tensor(name, shape, dt))
    es2 = ExitStack()
    SB2 = lambda name, shape, dt: es2.enter_context(nc.sbuf_tensor(name, shape, dt))
    PS2 = lambda name, shape: es2.enter_context(nc.psum_tensor(name, shape, F32))
    lnt = [SB2(f"lnt{i}", [128, NT], BF16) for i in range(NDCH)]
    wti = [SB2(f"wti{i}", [128, 2048], F32) for i in range(2)]
    wtb = [SB2(f"wtb{i}", [128, 2048], BF16) for i in range(2)]
    prj = [SB2(f"prj{i}", [128, 512], BF16) for i in range(2)]
    ps_p = [PS2(f"ps_p{i}", [128, 512]) for i in range(2)]

    for eng in ("tensor", "scalar", "gpsimd", "sync"):
        @p.on(eng)
        def _(e):
            e.wait_ge(ph.h, 1)

    lnld_marks = []
    for dch in range(NDCH):
        vq = p.dma("sync", lnt[dch][:], ln2d[dch * 128:(dch + 1) * 128, :], dsl[0],
                   waits=[(sln, ln_stores)] if dch == 0 else [])
        lnld_marks.append(vq)
    v_lnres = dsl[0].n

    NOCH = 48
    last_wcvt = [0, 0]
    last_prj = [0, 0]
    for och in range(NOCH):
        wp = och % 2
        src_w = wqkvT if och < 32 else wzT
        col0 = och * 128 if och < 32 else (och - 32) * 128
        lw2 = [(c_v, last_wcvt[wp])] if last_wcvt[wp] > 0 else []
        for dch in range(NDCH):
            p.dma("gpsimd", wti[wp][:, dch * 128:(dch + 1) * 128],
                  src_w[dch * 128:(dch + 1) * 128, col0:col0 + 128], dwt[wp],
                  waits=lw2 if dch == 0 else [])
        v_w = dwt[wp].n
        @p.on("vector")
        def _(v, wp=wp, v_w=v_w, wmm=mm.n):
            v.wait_ge(dwt[wp].h, v_w)
            if wmm > 0:
                v.wait_ge(mm.h, wmm)
            v.tensor_copy(out=wtb[wp][:], in_=wti[wp][:]).then_inc(c_v.h, 1)
        last_wcvt[wp] = c_v.inc()

        for tsub in range(8):
            pp = (och * 8 + tsub) % 2
            @p.on("tensor")
            def _(t, wp=wp, pp=pp, tsub=tsub, cw=last_wcvt[wp], v_lnres=v_lnres,
                  wpr=last_prj[pp]):
                t.wait_ge(c_v.h, cw)
                t.wait_ge(dsl[0].h, v_lnres)
                if wpr > 0:
                    t.wait_ge(ev.h, wpr)
                for dch in range(NDCH):
                    mmi = t.matmul(out=ps_p[pp][:], lhsT=wtb[wp][:, dch * 128:(dch + 1) * 128],
                                   rhs=lnt[dch][:, tsub * 512:(tsub + 1) * 512],
                                   start=(dch == 0), stop=(dch == NDCH - 1))
                mmi.then_inc(mm.h, 1)
            c_pm = mm.inc()
            @p.on("vector")
            def _(v, pp=pp, c_pm=c_pm, wsq=sq[pp].n):
                v.wait_ge(mm.h, c_pm)
                if wsq > 0:
                    v.wait_ge(sq[pp].h, wsq)
                v.tensor_copy(out=prj[pp][:], in_=ps_p[pp][:]).then_inc(ev.h, 1)
            last_prj[pp] = ev.inc()
            dst = (qpd if och < 32 else zd)
            r0 = och * 128 if och < 32 else (och - 32) * 128
            p.dma("gpsimd", dst[r0:r0 + 128, tsub * 512:(tsub + 1) * 512], prj[pp][:],
                  sq[pp], waits=[(ev, last_prj[pp])])

    wab_f = SB2("wab_f", [128, 512], F32)
    wab_b = SB2("wab_b", [128, 512], BF16)
    a_st = SB2("a_st", [16, 512], F32)
    b_st = SB2("b_st", [16, 512], F32)
    ps_a16 = PS2("ps_a16", [16, 512])
    ps_b16 = PS2("ps_b16", [16, 512])
    vw = p.dma("sync", wab_f[:].rearrange("p (a c) -> p a c", a=16), wabT.rearrange("(a p) c -> p a c", p=128), dwab)
    @p.on("vector")
    def _(v, vw=vw):
        v.wait_ge(dwab.h, vw)
        v.tensor_copy(out=wab_b[:], in_=wab_f[:]).then_inc(c_v.h, 1)
    c_wab = c_v.inc()
    for tsub in range(8):
        @p.on("tensor")
        def _(t, tsub=tsub, v_lnres=v_lnres, wev=ev.n, c_wab=c_wab):
            t.wait_ge(c_v.h, c_wab)
            t.wait_ge(dsl[0].h, v_lnres)
            t.wait_ge(ev.h, wev)
            for dch in range(NDCH):
                t.matmul(out=ps_a16[:], lhsT=wab_b[:, dch * 32:dch * 32 + 16],
                         rhs=lnt[dch][:, tsub * 512:(tsub + 1) * 512],
                         start=(dch == 0), stop=(dch == NDCH - 1))
            for dch in range(NDCH):
                mmi = t.matmul(out=ps_b16[:], lhsT=wab_b[:, dch * 32 + 16:dch * 32 + 32],
                               rhs=lnt[dch][:, tsub * 512:(tsub + 1) * 512],
                               start=(dch == 0), stop=(dch == NDCH - 1))
            mmi.then_inc(mm.h, 1)
        c_am = mm.inc()
        @p.on("vector")
        def _(v, tsub=tsub, c_am=c_am, wst=sab.n):
            v.wait_ge(mm.h, c_am)
            if wst > 0:
                v.wait_ge(sab.h, wst)
            v.tensor_copy(out=a_st[:], in_=ps_a16[:])
            v.tensor_copy(out=b_st[:], in_=ps_b16[:]).then_inc(ev.h, 1)
        c_abe = ev.inc()
        p.dma("gpsimd", abd[0:16, tsub * 512:(tsub + 1) * 512], a_st[:], sab,
              waits=[(ev, c_abe)])
        p.dma("gpsimd", abd[16:32, tsub * 512:(tsub + 1) * 512], b_st[:], sab,
              waits=[(ev, c_abe)])
    c_ab = ev.n
    n_sab = sab.n
    es2.close()

    gtok = [SB(f"gtok{i}", [128, 16], F32) for i in range(NCH)]
    btok = [SB(f"btok{i}", [128, 16], F32) for i in range(NCH)]
    egt = [SB(f"egt{i}", [128, 16], F32) for i in range(NCH)]
    negegt = [SB(f"negegt{i}", [128, 16], F32) for i in range(NCH)]
    amat = [SB(f"amat{i}", [128, 16], F32) for i in range(NCH)]
    glr = SB("glr", [1, 16], F32)
    es_ab = ExitStack()
    SBa = lambda name, shape, dt: es_ab.enter_context(nc.sbuf_tensor(name, shape, dt))
    a_t = SBa("a_t", [16, NT], F32)
    b_t = SBa("b_t", [16, NT], F32)
    g_t = SBa("g_t", [16, NT], F32)
    sp_t = SBa("sp_t", [16, NT], F32)
    va_ = p.dma("sync", a_t[:], abd[0:16, :], dab, waits=[(sab, n_sab)])
    vb_ = p.dma("sync", b_t[:], abd[16:32, :], dab)
    ps_tr = es_ab.enter_context(nc.psum_tensor("ps_tr", [128, 512], F32))

    @p.on("scalar")
    def _(sc, c_ab=c_ab, vb_=vb_):
        sc.wait_ge(ev.h, c_ab)
        sc.wait_ge(dab.h, vb_)
        sc.activation(out=sp_t[:], in_=a_t[:], func=AF.Abs, bias=dtb_t[:])
        sc.activation(out=sp_t[:], in_=sp_t[:], func=AF.Exp, scale=-1.0)
        sc.activation(out=sp_t[:], in_=sp_t[:], func=AF.Ln, bias=one1[:]).then_inc(c_s.h, 1)
    c_sp1 = c_s.inc()
    @p.on("vector")
    def _(v, c_sp1=c_sp1, c_nec=c_nec):
        v.wait_ge(c_s.h, c_sp1)
        v.wait_ge(c_v.h, c_nec)
        v.tensor_scalar(out=g_t[:], in0=a_t[:], scalar1=dtb_t[:], scalar2=0.0,
                        op0=ALU.add, op1=ALU.max)
        v.tensor_tensor(out=g_t[:], in0=g_t[:], in1=sp_t[:], op=ALU.add)
        v.tensor_scalar(out=g_t[:], in0=g_t[:], scalar1=nec_t[:], scalar2=None,
                        op0=ALU.mult).then_inc(c_v.h, 1)
    c_g0 = c_v.inc()
    @p.on("scalar")
    def _(sc, vb_=vb_):
        sc.wait_ge(dab.h, vb_)
        sc.activation(out=b_t[:], in_=b_t[:], func=AF.Sigmoid).then_inc(c_s.h, 1)
    c_beta = c_s.inc()
    zs16 = SBa("zs16", [16, 128], F32)
    @p.on("vector")
    def _(v, c_g0=c_g0):
        v.wait_ge(c_v.h, c_g0)
        v.memset(zs16[:], 0.0)
        for c in range(NCH):
            ti = v.tensor_tensor_scan(out=sp_t[:, c * 128:(c + 1) * 128],
                                      data0=g_t[:, c * 128:(c + 1) * 128],
                                      data1=zs16[:],
                                      initial=0.0, op0=ALU.add, op1=ALU.add)
        ti.then_inc(c_v.h, 1)
    c_gc = c_v.inc()
    p.dma("gpsimd", gcd, sp_t[:], sgt, waits=[(c_v, c_gc)])
    p.dma("gpsimd", btd, b_t[:], sgt, waits=[(c_s, c_beta)])
    for c in range(NCH):
        @p.on("tensor")
        def _(t, c=c, c_gc=c_gc, c_beta=c_beta, wev=ev.n):
            t.wait_ge(c_v.h, c_gc)
            t.wait_ge(c_s.h, c_beta)
            if wev > 0:
                t.wait_ge(ev.h, wev)
            t.transpose(out=ps_tr[:, 0:16], in_=sp_t[:, c * 128:(c + 1) * 128], identity=id_f[0:16, 0:16])
            t.transpose(out=ps_tr[:, 128:144], in_=b_t[:, c * 128:(c + 1) * 128],
                        identity=id_f[0:16, 0:16]).then_inc(mm.h, 1)
        c_tr = mm.inc()
        @p.on("vector")
        def _(v, c=c, c_tr=c_tr):
            v.wait_ge(mm.h, c_tr)
            v.tensor_copy(out=gtok[c][:], in_=ps_tr[:, 0:16])
            v.tensor_copy(out=btok[c][:], in_=ps_tr[:, 128:144]).then_inc(ev.h, 1)
        c_gt = ev.inc()
        @p.on("scalar")
        def _(sc, c=c, c_gt=c_gt):
            sc.wait_ge(ev.h, c_gt)
            sc.activation(out=egt[c][:], in_=gtok[c][:], func=AF.Exp).then_inc(c_s.h, 1)
        c_eg = c_s.inc()
        @p.on("vector")
        def _(v, c=c, c_eg=c_eg):
            v.wait_ge(c_s.h, c_eg)
            v.tensor_scalar(out=negegt[c][:], in0=egt[c][:], scalar1=-1.0, scalar2=None,
                            op0=ALU.mult).then_inc(ev.h, 1)
        ev.inc()
        p.dma("gpsimd", gtd[c * 128:(c + 1) * 128, :], gtok[c][:], sgt, waits=[(ev, c_gt)])
    c_c3 = ev.n
    n_sgt = sgt.n
    for c in range(NCH):
        vg = p.dma("sync", glr[:], gtd[c * 128 + 127:c * 128 + 128, :], dgl,
                   waits=[(sgt, n_sgt), (mm, mm.n)])
        @p.on("scalar")
        def _(sc, c=c, vg=vg):
            sc.wait_ge(dgl.h, vg)
            sc.activation(out=glr[:], in_=glr[:], func=AF.Exp).then_inc(c_s.h, 1)
        c_al = c_s.inc()
        @p.on("tensor")
        def _(t, c=c, c_al=c_al, wev=ev.n):
            t.wait_ge(c_s.h, c_al)
            t.wait_ge(ev.h, wev)
            t.matmul(out=ps_tr[:, 256:272], lhsT=onesf_row[:], rhs=glr[:],
                     start=True, stop=True).then_inc(mm.h, 1)
        c_alm = mm.inc()
        @p.on("vector")
        def _(v, c=c, c_alm=c_alm):
            v.wait_ge(mm.h, c_alm)
            v.tensor_copy(out=amat[c][:], in_=ps_tr[:, 256:272]).then_inc(ev.h, 1)
        ev.inc()
    c_amats = ev.n

    es_ab.close()
    es3 = ExitStack()
    SB3 = lambda name, shape, dt: es3.enter_context(nc.sbuf_tensor(name, shape, dt))
    cw_t = [SB3(f"cw_t{i}", [128, 4], F32) for i in range(2)]
    cin = [SB3(f"cin{i}", [128, NT], BF16) for i in range(2)]
    cvt = [SB3(f"cvt{i}", [128, NT], F32) for i in range(2)]
    cot = [SB3(f"cot{i}", [128, NT], BF16) for i in range(2)]
    conv_marks = []
    n_sq = (sq[0].n, sq[1].n)
    for chh in range(32):
        sl2 = chh % 2
        lw = [(sq[0], n_sq[0]), (sq[1], n_sq[1])] if chh < 2 else [(c_s, conv_marks[chh - 2])]
        p.dma("sync", cin[sl2][:], qpd[chh * 128:(chh + 1) * 128, :], dq[sl2], waits=lw)
        vw2 = p.dma("sync", cw_t[sl2][:], convw[chh * 128:(chh + 1) * 128, :], dq[sl2])
        eng = "vector"
        cm = c_v
        @p.on(eng)
        def _(e, sl2=sl2, vw2=vw2, wcv=scv.n):
            e.wait_ge(dq[sl2].h, vw2)
            if wcv > 0:
                e.wait_ge(scv.h, wcv)
            e.tensor_scalar(out=cvt[sl2][:], in0=cin[sl2][:], scalar1=cw_t[sl2][:, 3:4],
                            scalar2=None, op0=ALU.mult)
            e.scalar_tensor_tensor(out=cvt[sl2][:, 1:NT], in0=cin[sl2][:, 0:NT - 1],
                                   scalar=cw_t[sl2][:, 2:3], in1=cvt[sl2][:, 1:NT],
                                   op0=ALU.mult, op1=ALU.add)
            e.scalar_tensor_tensor(out=cvt[sl2][:, 2:NT], in0=cin[sl2][:, 0:NT - 2],
                                   scalar=cw_t[sl2][:, 1:2], in1=cvt[sl2][:, 2:NT],
                                   op0=ALU.mult, op1=ALU.add)
            e.scalar_tensor_tensor(out=cvt[sl2][:, 3:NT], in0=cin[sl2][:, 0:NT - 3],
                                   scalar=cw_t[sl2][:, 0:1], in1=cvt[sl2][:, 3:NT],
                                   op0=ALU.mult, op1=ALU.add).then_inc(cm.h, 1)
        c_cv = cm.inc()
        @p.on("scalar")
        def _(sc, sl2=sl2, c_cv=c_cv, cm=cm):
            sc.wait_ge(cm.h, c_cv)
            sc.activation(out=cot[sl2][:], in_=cvt[sl2][:], func=AF.Silu).then_inc(c_s.h, 1)
        c_si = c_s.inc()
        conv_marks.append(c_si)
        p.dma("gpsimd", qcd[chh * 128:(chh + 1) * 128, :], cot[sl2][:], scv,
              waits=[(c_s, c_si)])
    n_scv = scv.n
    es3.close()

    s_k2c = SB("s_k2c", [128, 512], F32)
    es4 = ExitStack()
    SB4 = lambda name, shape, dt: es4.enter_context(nc.sbuf_tensor(name, shape, dt))
    PS4 = lambda name, shape, dt=F32: es4.enter_context(nc.psum_tensor(name, shape, dt))
    qf_t = SB4("qf_t", [128, NT], BF16)
    kf_t = SB4("kf_t", [128, NT], BF16)
    vload = SB4("vload", [128, NT], BF16)
    ktok = SB4("ktok", [128, NT], BF16)
    vtok = SB4("vtok", [128, NT], BF16)
    ssq512 = SB4("ssq512", [1, 512], F32)
    grow = SB4("grow", [1, NT], F32)
    gf_b = SB4("gf_b", [128, NT], F32)
    d_b = SB4("d_b", [128, NT], F32)
    eup_b = SB4("eup_b", [128, NT], F32)
    elow_b = SB4("elow_b", [128, NT], F32)
    bpart_b = SB4("bpart_b", [128, NT], F32)
    mu4 = SB4("mu4", [128, 512], F32)
    ml4 = SB4("ml4", [128, 512], F32)
    id4_f = SB4("id4_f", [128, 512], F32)
    tmp4 = s_k2c
    msb = SB4("msb", [128, 512], BF16)
    mtsb = SB4("mtsb", [128, 512], BF16)
    attnT_sb = SB4("attnT_sb", [128, NT], BF16)
    ainvT_sb = SB4("ainvT_sb", [128, NT], BF16)
    mpow = [SB4(f"mpow{i}", [128, 512], BF16) for i in range(6)]
    mtpow = [SB4(f"mtpow{i}", [128, 512], BF16) for i in range(2)]
    pchain = SB4("pchain", [128, 512], BF16)
    state_f = [SB4(f"state_f{i}", [128, 128], F32) for i in range(2)]
    state_b = SB4("state_b", [128, 128], BF16)
    r2a = SB4("r2a", [128, 128], F32)
    r2b = SB4("r2b", [128, 128], BF16)
    vnew_sb = SB4("vnew_sb", [128, 128], BF16)
    kdec_sb = SB4("kdec_sb", [128, 128], BF16)
    coslab = SB4("coslab", [128, NT], F32)
    psk_bf = SB4("psk_bf", [128, 128], BF16)
    egf128 = SB4("egf128", [128, 128], BF16)
    qeg = SB4("qeg", [128, 128], BF16)

    ps_sym = PS("ps_sym", [128, 512])
    ps_g2 = PS4("ps_g2", [128, 512])
    ps_trb = es4.enter_context(nc.psum_tensor("ps_trb", [128, 128], BF16))
    ps_sq2 = [PS4(f"ps_sq2{i}", [128, 512]) for i in range(2)]
    ps_pch = PS4("ps_pch", [128, 512])
    ps_kst = PS4("ps_kst", [128, 128])
    ps_out = PS4("ps_out", [128, 128])

    @p.on("vector")
    def _(v):
        for j in range(4):
            v.tensor_copy(out=mu4[:, j * 128:(j + 1) * 128], in_=mu_t[:])
            v.tensor_copy(out=ml4[:, j * 128:(j + 1) * 128], in_=ml_t[:])
            ti = v.tensor_copy(out=id4_f[:, j * 128:(j + 1) * 128], in_=id_f[:])
        ti.then_inc(c_v.h, 1)
    c_masks = c_v.inc()

    prev_hv_ev = 0
    prev_hv_cs = 0
    for kq in range(NKQ):
        vq_ = p.dma("sync", qf_t[:], qcd[kq * 128:(kq + 1) * 128, :], dq[0],
                    waits=[(scv, n_scv), (ev, prev_hv_ev), (c_s, prev_hv_cs)])
        vk_ = p.dma("sync", kf_t[:], qcd[(8 + kq) * 128:(8 + kq) * 128 + 128, :], dq[0])
        for (buf, bias_t) in ((qf_t, lnc_t), (kf_t, None)):
            for su2 in range(8):
                @p.on("vector")
                def _(v, buf=buf, su2=su2, vk_=vk_, wmm=mm.n):
                    v.wait_ge(dq[0].h, vk_)
                    if wmm > 0:
                        v.wait_ge(mm.h, wmm)
                    v.tensor_tensor(out=s_k2c[:], in0=buf[:, su2 * 512:(su2 + 1) * 512],
                                    in1=buf[:, su2 * 512:(su2 + 1) * 512], op=ALU.mult).then_inc(c_v.h, 1)
                c_k2 = c_v.inc()
                @p.on("tensor")
                def _(t, c_k2=c_k2, wcs=c_s.n):
                    t.wait_ge(c_v.h, c_k2)
                    if wcs > 0:
                        t.wait_ge(c_s.h, wcs)
                    t.matmul(out=ps_sym[0:1, :], lhsT=onesf_col[:], rhs=s_k2c[:],
                             start=True, stop=True).then_inc(mm.h, 1)
                c_nm = mm.inc()
                @p.on("scalar")
                def _(sc, bias_t=bias_t, c_nm=c_nm):
                    sc.wait_ge(mm.h, c_nm)
                    sc.activation(out=ssq512[:], in_=ps_sym[0:1, :], func=AF.Ln, bias=eps6[:])
                    if bias_t is not None:
                        sc.activation(out=ssq512[:], in_=ssq512[:], func=AF.Exp, scale=-0.5,
                                      bias=bias_t[:]).then_inc(c_s.h, 1)
                    else:
                        sc.activation(out=ssq512[:], in_=ssq512[:], func=AF.Exp,
                                      scale=-0.5).then_inc(c_s.h, 1)
                c_rr = c_s.inc()
                @p.on("tensor")
                def _(t, c_rr=c_rr, wev=ev.n):
                    t.wait_ge(c_s.h, c_rr)
                    if wev > 0:
                        t.wait_ge(ev.h, wev)
                    t.matmul(out=ps_g2[:], lhsT=onesf_row[:], rhs=ssq512[:],
                             start=True, stop=True).then_inc(mm.h, 1)
                c_bc = mm.inc()
                @p.on("vector")
                def _(v, buf=buf, su2=su2, c_bc=c_bc):
                    v.wait_ge(mm.h, c_bc)
                    v.tensor_copy(out=s_k2c[:], in_=ps_g2[:])
                    v.tensor_tensor(out=buf[:, su2 * 512:(su2 + 1) * 512],
                                    in0=buf[:, su2 * 512:(su2 + 1) * 512],
                                    in1=s_k2c[:], op=ALU.mult).then_inc(ev.h, 1)
                ev.inc()
        c_qk = ev.n
        for c in range(NCH):
            @p.on("tensor")
            def _(t, c=c, c_qk=c_qk, wev=ev.n):
                t.wait_ge(ev.h, max(c_qk, wev))
                t.transpose(out=ps_trb[:], in_=kf_t[:, c * 128:(c + 1) * 128],
                            identity=id_bf[:]).then_inc(mm.h, 1)
            c_t = mm.inc()
            @p.on("vector")
            def _(v, c=c, c_t=c_t):
                v.wait_ge(mm.h, c_t)
                v.tensor_copy(out=ktok[:, c * 128:(c + 1) * 128], in_=ps_trb[:]).then_inc(ev.h, 1)
            ev.inc()

        for hvl in (2 * kq, 2 * kq + 1):
            hp = hvl % 2
            vv = p.dma("sync", vload[:], qcd[(16 + hvl) * 128:(16 + hvl) * 128 + 128, :],
                       dv_[hp], waits=[(ev, prev_hv_ev), (sco, sco.n)])
            for c in range(NCH):
                @p.on("tensor")
                def _(t, c=c, vv=vv, hp=hp, wev=ev.n):
                    t.wait_ge(dv_[hp].h, vv)
                    t.wait_ge(ev.h, wev)
                    t.transpose(out=ps_trb[:], in_=vload[:, c * 128:(c + 1) * 128],
                                identity=id_bf[:]).then_inc(mm.h, 1)
                c_t = mm.inc()
                @p.on("vector")
                def _(v, c=c, c_t=c_t):
                    v.wait_ge(mm.h, c_t)
                    v.tensor_copy(out=vtok[:, c * 128:(c + 1) * 128], in_=ps_trb[:]).then_inc(ev.h, 1)
                ev.inc()
            vbr = p.dma("sync", grow[:], gcd[hvl:hvl + 1, :], dgr,
                        waits=[(sgt, n_sgt), (ev, ev.n)])
            for su2 in range(8):
                @p.on("tensor")
                def _(t, su2=su2, vbr=vbr, wev=ev.n):
                    t.wait_ge(dgr.h, vbr)
                    t.wait_ge(ev.h, wev)
                    t.matmul(out=ps_sym[:], lhsT=onesf_row[:],
                             rhs=grow[:, su2 * 512:(su2 + 1) * 512],
                             start=True, stop=True).then_inc(mm.h, 1)
                c_bc = mm.inc()
                @p.on("vector")
                def _(v, su2=su2, c_bc=c_bc):
                    v.wait_ge(mm.h, c_bc)
                    v.tensor_copy(out=gf_b[:, su2 * 512:(su2 + 1) * 512],
                                  in_=ps_sym[:]).then_inc(ev.h, 1)
                ev.inc()
            @p.on("vector")
            def _(v, hvl=hvl, wev=ev.n, c_masks=c_masks):
                v.wait_ge(ev.h, wev)
                v.wait_ge(c_v.h, c_masks)
                for c in range(NCH):
                    sl = slice(c * 128, (c + 1) * 128)
                    v.scalar_tensor_tensor(out=d_b[:, sl], in0=gf_b[:, sl],
                                           scalar=gtok[c][:, hvl:hvl + 1], in1=mu_t[:],
                                           op0=ALU.subtract, op1=ALU.mult)
                    ti = v.tensor_scalar(out=bpart_b[:, sl], in0=ones_t[:],
                                         scalar1=btok[c][:, hvl:hvl + 1], scalar2=None,
                                         op0=ALU.mult)
                ti.then_inc(c_v.h, 1)
            c_db = c_v.inc()
            @p.on("scalar")
            def _(sc, c_db=c_db):
                sc.wait_ge(c_v.h, c_db)
                sc.activation(out=eup_b[:], in_=d_b[:], func=AF.Exp).then_inc(c_s.h, 1)
            c_eup = c_s.inc()
            @p.on("vector")
            def _(v, hvl=hvl, c_eup=c_eup):
                v.wait_ge(c_s.h, c_eup)
                for c in range(NCH):
                    sl = slice(c * 128, (c + 1) * 128)
                    ti = v.scalar_tensor_tensor(out=elow_b[:, sl], in0=gf_b[:, sl],
                                                scalar=gtok[c][:, hvl:hvl + 1], in1=ml_t[:],
                                                op0=ALU.subtract, op1=ALU.mult)
                ti.then_inc(c_v.h, 1)
            c_dl = c_v.inc()
            @p.on("scalar")
            def _(sc, c_dl=c_dl):
                sc.wait_ge(c_v.h, c_dl)
                sc.activation(out=elow_b[:], in_=elow_b[:], func=AF.Exp,
                              scale=-1.0).then_inc(c_s.h, 1)
            c_elow = c_s.inc()

            for grp in range(8):
                c0 = grp * 4
                gsl = slice(c0 * 128, (c0 + 4) * 128)
                @p.on("tensor")
                def _(t, c0=c0, c_elow=c_elow, wev=ev.n, wcv=c_v.n):
                    t.wait_ge(c_s.h, c_elow)
                    t.wait_ge(ev.h, wev)
                    t.wait_ge(c_v.h, wcv)
                    for j in range(4):
                        t.matmul(out=ps_sym[:, j * 128:(j + 1) * 128],
                                 lhsT=kf_t[:, (c0 + j) * 128:(c0 + j + 1) * 128],
                                 rhs=kf_t[:, (c0 + j) * 128:(c0 + j + 1) * 128],
                                 start=True, stop=True)
                    for j in range(4):
                        mmi = t.matmul(out=ps_g2[:, j * 128:(j + 1) * 128],
                                       lhsT=kf_t[:, (c0 + j) * 128:(c0 + j + 1) * 128],
                                       rhs=qf_t[:, (c0 + j) * 128:(c0 + j + 1) * 128],
                                       start=True, stop=True)
                    mmi.then_inc(mm.h, 1)
                c_sym = mm.inc()
                @p.on("vector")
                def _(v, gsl=gsl, c_sym=c_sym):
                    v.wait_ge(mm.h, c_sym)
                    v.tensor_tensor(out=tmp4[:], in0=ps_sym[:], in1=bpart_b[:, gsl], op=ALU.mult)
                    v.tensor_tensor(out=tmp4[:], in0=tmp4[:], in1=elow_b[:, gsl], op=ALU.mult)
                    v.tensor_tensor(out=msb[:], in0=tmp4[:], in1=ml4[:], op=ALU.mult)
                    v.tensor_tensor(out=tmp4[:], in0=ps_g2[:], in1=eup_b[:, gsl], op=ALU.mult)
                    v.tensor_tensor(out=attnT_sb[:, gsl], in0=tmp4[:], in1=mu4[:],
                                    op=ALU.mult).then_inc(c_v.h, 1)
                c_mb = c_v.inc()
                for j in range(4):
                    @p.on("tensor")
                    def _(t, j=j, c_mb=c_mb, wev=ev.n):
                        t.wait_ge(c_v.h, c_mb)
                        if j > 0:
                            t.wait_ge(ev.h, wev)
                        t.transpose(out=ps_trb[:], in_=msb[:, j * 128:(j + 1) * 128],
                                    identity=id_bf[:]).then_inc(mm.h, 1)
                    c_t = mm.inc()
                    @p.on("vector")
                    def _(v, j=j, c_t=c_t):
                        v.wait_ge(mm.h, c_t)
                        v.tensor_copy(out=mtsb[:, j * 128:(j + 1) * 128],
                                      in_=ps_trb[:]).then_inc(ev.h, 1)
                    ev.inc()
                c_mt = ev.n
                curM, curMT = msb, mtsb
                c_prev = (c_v, c_mb)
                for lvl in range(6):
                    @p.on("tensor")
                    def _(t, curM=curM, curMT=curMT, c_prev=c_prev, c_mt=c_mt, wev=ev.n,
                          wcs=c_s.n, wcv=c_v.n):
                        t.wait_ge(c_prev[0].h, c_prev[1])
                        t.wait_ge(ev.h, max(c_mt, wev))
                        t.wait_ge(c_s.h, wcs)
                        t.wait_ge(c_v.h, wcv)
                        for j in range(4):
                            js = slice(j * 128, (j + 1) * 128)
                            t.matmul(out=ps_sq2[0][:, js], lhsT=curMT[:, js], rhs=curM[:, js],
                                     start=True, stop=True)
                        for j in range(4):
                            js = slice(j * 128, (j + 1) * 128)
                            mmi = t.matmul(out=ps_sq2[1][:, js], lhsT=curM[:, js], rhs=curMT[:, js],
                                           start=True, stop=True)
                        mmi.then_inc(mm.h, 1)
                    c_sq = mm.inc()
                    @p.on("scalar")
                    def _(sc, lvl=lvl, c_sq=c_sq):
                        sc.wait_ge(mm.h, c_sq)
                        sc.activation(out=mpow[lvl][:], in_=ps_sq2[0][:], func=AF.Copy).then_inc(c_s.h, 1)
                    c_e1 = c_s.inc()
                    @p.on("vector")
                    def _(v, lvl=lvl, c_sq=c_sq):
                        v.wait_ge(mm.h, c_sq)
                        v.tensor_copy(out=mtpow[lvl % 2][:], in_=ps_sq2[1][:]).then_inc(c_v.h, 1)
                    c_e2 = c_v.inc()
                    curM, curMT = mpow[lvl], mtpow[lvl % 2]
                    c_prev = (c_v, c_e2)
                    c_prev_s = (c_s, c_e1)
                @p.on("scalar")
                def _(sc, c_mt=c_mt, wmm=mm.n):
                    sc.wait_ge(ev.h, c_mt)
                    sc.wait_ge(mm.h, wmm)
                    sc.activation(out=ps_pch[:], in_=mtsb[:], func=AF.Copy,
                                  scale=-1.0).then_inc(c_s.h, 1)
                c_p0 = c_s.inc()
                @p.on("vector")
                def _(v, c_p0=c_p0):
                    v.wait_ge(c_s.h, c_p0)
                    v.tensor_tensor(out=ps_pch[:], in0=ps_pch[:], in1=id4_f[:],
                                    op=ALU.add).then_inc(c_v.h, 1)
                c_p1 = c_v.inc()
                cpe = (c_v, c_p1)
                for k in range(6):
                    if k % 2 == 0:
                        @p.on("scalar")
                        def _(sc, cpe=cpe, k=k):
                            sc.wait_ge(cpe[0].h, cpe[1])
                            sc.activation(out=pchain[:], in_=ps_pch[:], func=AF.Copy).then_inc(c_s.h, 1)
                        cpv = (c_s, c_s.inc())
                    else:
                        @p.on("vector")
                        def _(v, cpe=cpe, k=k):
                            v.wait_ge(cpe[0].h, cpe[1])
                            v.tensor_copy(out=pchain[:], in_=ps_pch[:]).then_inc(c_v.h, 1)
                        cpv = (c_v, c_v.inc())
                    @p.on("tensor")
                    def _(t, k=k, cpv=cpv):
                        t.wait_ge(cpv[0].h, cpv[1])
                        for j in range(4):
                            js = slice(j * 128, (j + 1) * 128)
                            mmi = t.matmul(out=ps_pch[:, js], lhsT=mpow[k][:, js],
                                           rhs=pchain[:, js], start=False, stop=True)
                        mmi.then_inc(mm.h, 1)
                    cpe = (mm, mm.inc())
                @p.on("scalar")
                def _(sc, gsl=gsl, cpe=cpe):
                    sc.wait_ge(cpe[0].h, cpe[1])
                    sc.activation(out=ainvT_sb[:, gsl], in_=ps_pch[:], func=AF.Copy).then_inc(c_s.h, 1)
                c_s.inc()
            c_stageA = c_s.n
            c_stageA_v = c_v.n

            @p.on("vector")
            def _(v, c_stageA=c_stageA):
                v.wait_ge(c_s.h, c_stageA)
                v.memset(state_f[0][:], 0.0)
                v.memset(state_f[1][:], 0.0)
                v.memset(state_b[:], 0.0).then_inc(c_v.h, 1)
            c_st0 = c_v.inc()
            last_cast = (c_v, c_st0)
            kdec_war = mm.n
            for c in range(NCH):
                sp_, sn_ = c % 2, 1 - c % 2
                @p.on("tensor")
                def _(t, c=c, last_cast=last_cast, wcs=c_s.n):
                    t.wait_ge(last_cast[0].h, last_cast[1])
                    t.wait_ge(c_s.h, wcs)
                    t.matmul(out=ps_kst[:], lhsT=kf_t[:, c * 128:(c + 1) * 128],
                             rhs=state_b[:], start=True, stop=True).then_inc(mm.h, 1)
                c_m1 = mm.inc()
                @p.on("scalar")
                def _(sc, c_m1=c_m1):
                    sc.wait_ge(mm.h, c_m1)
                    sc.activation(out=psk_bf[:], in_=ps_kst[:], func=AF.Copy).then_inc(c_s.h, 1)
                c_e1 = c_s.inc()
                @p.on("vector")
                def _(v, c=c, hvl=hvl, c_e1=c_e1, kdec_war=kdec_war):
                    if kdec_war > 0:
                        v.wait_ge(mm.h, kdec_war)
                    v.tensor_scalar(out=kdec_sb[:], in0=ktok[:, c * 128:(c + 1) * 128],
                                    scalar1=eup_b[:, c * 128 + 127:c * 128 + 128],
                                    scalar2=None, op0=ALU.mult).then_inc(c_v.h, 1)
                    v.wait_ge(c_s.h, c_e1)
                    v.scalar_tensor_tensor(out=r2a[:], in0=psk_bf[:],
                                           scalar=negegt[c][:, hvl:hvl + 1],
                                           in1=vtok[:, c * 128:(c + 1) * 128],
                                           op0=ALU.mult, op1=ALU.add)
                    v.tensor_scalar(out=r2b[:], in0=r2a[:], scalar1=btok[c][:, hvl:hvl + 1],
                                    scalar2=None, op0=ALU.mult).then_inc(c_v.h, 1)
                c_kd = c_v.inc()
                c_r2 = c_v.inc()
                @p.on("tensor")
                def _(t, c=c, c_r2=c_r2):
                    t.wait_ge(c_v.h, c_r2)
                    t.matmul(out=ps_kst[:], lhsT=ainvT_sb[:, c * 128:(c + 1) * 128],
                             rhs=r2b[:], start=True, stop=True).then_inc(mm.h, 1)
                c_m2 = mm.inc()
                @p.on("scalar")
                def _(sc, c_m2=c_m2):
                    sc.wait_ge(mm.h, c_m2)
                    sc.activation(out=vnew_sb[:], in_=ps_kst[:], func=AF.Copy).then_inc(c_s.h, 1)
                c_vn = c_s.inc()
                @p.on("scalar")
                def _(sc, c=c, c_vn=c_vn, wcv=c_v.n):
                    sc.wait_ge(c_v.h, wcv)
                    sc.activation(out=egf128[:], in_=gf_b[:, c * 128:(c + 1) * 128],
                                  func=AF.Exp).then_inc(c_s.h, 1)
                c_egf = c_s.inc()
                @p.on("vector")
                def _(v, c=c, c_egf=c_egf, wmm=mm.n):
                    v.wait_ge(c_s.h, c_egf)
                    if wmm > 0:
                        v.wait_ge(mm.h, wmm)
                    v.tensor_tensor(out=qeg[:], in0=qf_t[:, c * 128:(c + 1) * 128],
                                    in1=egf128[:], op=ALU.mult).then_inc(c_v.h, 1)
                c_qeg = c_v.inc()
                @p.on("tensor")
                def _(t, c=c, c_kd=c_kd, c_vn=c_vn):
                    t.wait_ge(c_v.h, c_kd)
                    t.wait_ge(c_s.h, c_vn)
                    t.matmul(out=ps_kst[:], lhsT=kdec_sb[:], rhs=vnew_sb[:],
                             start=True, stop=True).then_inc(mm.h, 1)
                c_m5 = mm.inc()
                kdec_war = c_m5
                @p.on("tensor")
                def _(t, c=c, c_qeg=c_qeg):
                    t.wait_ge(c_v.h, c_qeg)
                    t.matmul(out=ps_out[:], lhsT=state_b[:], rhs=qeg[:],
                             start=True, stop=False)
                    t.matmul(out=ps_out[:], lhsT=vnew_sb[:],
                             rhs=attnT_sb[:, c * 128:(c + 1) * 128],
                             start=False, stop=True).then_inc(mm.h, 1)
                c_m4 = mm.inc()
                @p.on("scalar")
                def _(sc, c=c, c_m4=c_m4, sco_war=sco.n):
                    sc.wait_ge(mm.h, c_m4)
                    if c == 0 and sco_war > 0:
                        sc.wait_ge(sco.h, sco_war)
                    sc.activation(out=coslab[:, c * 128:(c + 1) * 128], in_=ps_out[:],
                                  func=AF.Copy).then_inc(c_s.h, 1)
                c_s.inc()
                @p.on("vector")
                def _(v, c=c, hvl=hvl, sp_=sp_, sn_=sn_, c_m5=c_m5):
                    v.wait_ge(mm.h, c_m5)
                    v.scalar_tensor_tensor(out=state_f[sn_][:], in0=state_f[sp_][:],
                                           scalar=amat[c][:, hvl:hvl + 1], in1=ps_kst[:],
                                           op0=ALU.mult, op1=ALU.add).then_inc(c_v.h, 1)
                c_su = c_v.inc()
                @p.on("scalar")
                def _(sc, sn_=sn_, c_su=c_su, c_m4=c_m4):
                    sc.wait_ge(c_v.h, c_su)
                    sc.wait_ge(mm.h, c_m4)
                    sc.activation(out=state_b[:], in_=state_f[sn_][:], func=AF.Copy).then_inc(c_s.h, 1)
                last_cast = (c_s, c_s.inc())
            p.dma("gpsimd", cod[hvl * 128:(hvl + 1) * 128, :], coslab[:], sco,
                  waits=[(c_s, c_s.n)])
            prev_hv_ev = ev.n
            prev_hv_cs = c_s.n

    es4.close()
    ssq_row = SB("ssq_row6", [1, 512], F32)
    gf_b6 = SB("gf_b6", [128, NT], F32)
    ps_po = [PS(f"ps_po{i}", [128, 512]) for i in range(2)]
    n_sco = sco.n
    co_t = [SB(f"co_t{i}", [128, NT], F32) for i in range(2)]
    z_t = [SB(f"z_t{i}", [128, NT], BF16) for i in range(2)]
    gat = [SB(f"gat{i}", [128, NT], BF16) for i in range(2)]
    g6_marks = []
    for hvl in range(16):
        hp = hvl % 2
        lw = [(sco, n_sco)] if hvl < 2 else [(sgt, g6_marks[hvl - 2])]
        vc = p.dma("sync", co_t[hp][:], cod[hvl * 128:(hvl + 1) * 128, :], dco[hp], waits=lw)
        vz = p.dma("sync", z_t[hp][:], zd[hvl * 128:(hvl + 1) * 128, :], dz[hp])
        @p.on("scalar")
        def _(sc, hp=hp, vz=vz):
            sc.wait_ge(dz[hp].h, vz)
            sc.activation(out=z_t[hp][:], in_=z_t[hp][:], func=AF.Silu).then_inc(c_s.h, 1)
        c_zs = c_s.inc()
        for su2 in range(8):
            @p.on("vector")
            def _(v, hp=hp, su2=su2, vz=vz, wmm=mm.n):
                v.wait_ge(dz[hp].h, vz)
                if wmm > 0:
                    v.wait_ge(mm.h, wmm)
                v.tensor_tensor(out=s_k2c[:], in0=co_t[hp][:, su2 * 512:(su2 + 1) * 512],
                                in1=co_t[hp][:, su2 * 512:(su2 + 1) * 512], op=ALU.mult).then_inc(c_v.h, 1)
            c_k2 = c_v.inc()
            @p.on("tensor")
            def _(t, c_k2=c_k2, wcs=c_s.n):
                t.wait_ge(c_v.h, c_k2)
                t.wait_ge(c_s.h, wcs)
                t.matmul(out=ps_sym[0:1, :], lhsT=onesf_col[:], rhs=s_k2c[:],
                         start=True, stop=True).then_inc(mm.h, 1)
            c_nm = mm.inc()
            @p.on("scalar")
            def _(sc, c_nm=c_nm):
                sc.wait_ge(mm.h, c_nm)
                sc.activation(out=ssq_row[:], in_=ps_sym[0:1, :], func=AF.Ln, bias=eps6[:],
                              scale=1.0 / 128)
                sc.activation(out=ssq_row[:], in_=ssq_row[:], func=AF.Exp,
                              scale=-0.5).then_inc(c_s.h, 1)
            c_rr = c_s.inc()
            @p.on("tensor")
            def _(t, c_rr=c_rr, wev=ev.n):
                t.wait_ge(c_s.h, c_rr)
                if wev > 0:
                    t.wait_ge(ev.h, wev)
                t.matmul(out=ps_po[0][:], lhsT=onesf_row[:], rhs=ssq_row[:],
                         start=True, stop=True).then_inc(mm.h, 1)
            c_bc = mm.inc()
            @p.on("vector")
            def _(v, hp=hp, su2=su2, c_bc=c_bc, c_zs=c_zs):
                v.wait_ge(mm.h, c_bc)
                v.wait_ge(c_s.h, c_zs)
                sl = slice(su2 * 512, (su2 + 1) * 512)
                v.tensor_copy(out=s_k2c[:], in_=ps_po[0][:])
                v.tensor_tensor(out=co_t[hp][:, sl], in0=co_t[hp][:, sl], in1=s_k2c[:], op=ALU.mult)
                v.tensor_scalar(out=co_t[hp][:, sl], in0=co_t[hp][:, sl], scalar1=nw_t[:],
                                scalar2=None, op0=ALU.mult)
                v.tensor_copy(out=gf_b6[:, sl], in_=z_t[hp][:, sl])
                v.tensor_tensor(out=gat[hp][:, sl], in0=co_t[hp][:, sl], in1=gf_b6[:, sl],
                                op=ALU.mult).then_inc(ev.h, 1)
            ev.inc()
        c_g6 = ev.n
        vv = p.dma("gpsimd", gtd2[hvl * 128:(hvl + 1) * 128, :], gat[hp][:], sgt,
                   waits=[(ev, c_g6)])
        g6_marks.append(vv)
    n_sgt2 = sgt.n

    wo_f2 = SB("wo_f2", [128, D], F32)
    wo_b2 = [SB(f"wo_b2{i}", [128, D], BF16) for i in range(16)]
    gsl_t = [SB(f"gsl_t{i}", [128, 512 * 16], BF16) for i in range(2)]
    po_ev = [SB(f"po_ev{i}", [128, 512], F32) for i in range(2)]
    for zch in range(16):
        vw = p.dma("sync", wo_f2[:], woT[zch * 128:(zch + 1) * 128, :], dwo,
                   waits=[(c_v, c_v.n)])
        @p.on("vector")
        def _(v, zch=zch, vw=vw):
            v.wait_ge(dwo.h, vw)
            v.tensor_copy(out=wo_b2[zch][:], in_=wo_f2[:]).then_inc(c_v.h, 1)
        c_v.inc()
    c_wo = c_v.n
    last_gsl = [0, 0]
    for tsub in range(8):
        slw = tsub % 2
        lw = [(mm, last_gsl[slw])] if tsub >= 2 else [(sgt, n_sgt2)]
        for zch in range(16):
            p.dma("sync" if zch % 2 == 0 else "scalar",
                  gsl_t[slw][:, zch * 512:(zch + 1) * 512],
                  gtd2[zch * 128:(zch + 1) * 128, tsub * 512:(tsub + 1) * 512],
                  dgs[slw], waits=lw if zch <= 1 else [])
        v_sl = dgs[slw].n
        for ech in range(NDCH):
            pp = ech % 2
            gi6 = tsub * NDCH + ech
            @p.on("tensor")
            def _(t, ech=ech, pp=pp, slw=slw, v_sl=v_sl, c_wo=c_wo, wev=ev.n):
                t.wait_ge(c_v.h, c_wo)
                t.wait_ge(dgs[slw].h, v_sl)
                if wev > 0:
                    t.wait_ge(ev.h, max(wev - 1, 0))
                for zch in range(16):
                    mmi = t.matmul(out=ps_po[pp][:], lhsT=wo_b2[zch][:, ech * 128:(ech + 1) * 128],
                                   rhs=gsl_t[slw][:, zch * 512:(zch + 1) * 512],
                                   start=(zch == 0), stop=(zch == 15))
                mmi.then_inc(mm.h, 1)
            c_po = mm.inc()
            last_gsl[slw] = c_po
            @p.on("vector")
            def _(v, pp=pp, c_po=c_po, gi6=gi6):
                v.wait_ge(mm.h, c_po)
                if gi6 >= 2:
                    v.wait_ge(spo.h, 16 * (gi6 - 1))
                v.tensor_copy(out=po_ev[gi6 % 2][:], in_=ps_po[pp][:]).then_inc(ev.h, 1)
            c_pe = ev.inc()
            p.dma("gpsimd", partial[ech * 128:(ech + 1) * 128, tsub * 512:(tsub + 1) * 512],
                  po_ev[gi6 % 2][:], spo, waits=[(ev, c_pe)])

    @p.on("gpsimd")
    def _(g, f1=spo.n):
        g.wait_ge(spo.h, f1)

    p.emit()
    es.close()
    return nc


def prep_C_core(x2_b_T, hh, inputs):
    import ml_dtypes
    wqkv = inputs["L1_attn_W_qkv"]
    q0 = hh * 1024
    v0 = 4096 + hh * 2048
    rows = np.concatenate([
        wqkv[q0:q0 + 1024], wqkv[2048 + q0:2048 + q0 + 1024], wqkv[v0:v0 + 2048]], axis=0)
    wqkvT = np.ascontiguousarray(rows.T)
    cw = inputs["L1_attn_conv_w"][:, 0, :]
    convw = np.ascontiguousarray(np.concatenate([
        cw[q0:q0 + 1024], cw[2048 + q0:2048 + q0 + 1024], cw[v0:v0 + 2048]], axis=0))
    wzT = np.ascontiguousarray(inputs["L1_attn_W_z"][hh * 2048:(hh + 1) * 2048].T)
    wa = inputs["L1_attn_W_a"][hh * 16:(hh + 1) * 16]
    wb = inputs["L1_attn_W_b"][hh * 16:(hh + 1) * 16]
    wabT = np.ascontiguousarray(np.stack(
        [np.concatenate([wa[:, d:d+1], wb[:, d:d+1]], axis=0) for d in range(D)], axis=0
    ).reshape(D, 32)) if False else None
    wabT = np.ascontiguousarray(np.concatenate([wa, wb], axis=0).T)
    woT = np.ascontiguousarray(inputs["L1_attn_W_out"][:, hh * 2048:(hh + 1) * 2048].T)
    jj = np.arange(128)[:, None]
    ff = np.arange(128)[None, :]
    masklow = (jj > ff).astype(np.float32)
    maskup = (jj <= ff).astype(np.float32)
    crow, ccol = _consts()
    crow[2, :] = np.log(1.0 / np.sqrt(128.0))
    return {
        "x2Tf": x2_b_T, "wqkvT": wqkvT, "wzT": wzT, "wabT": wabT, "convw": convw,
        "dtb": np.ascontiguousarray(inputs["L1_attn_dt_bias"][hh * 16:(hh + 1) * 16].reshape(16, 1)),
        "alog": np.ascontiguousarray(inputs["L1_attn_A_log"][hh * 16:(hh + 1) * 16].reshape(16, 1)),
        "nweight": np.ascontiguousarray(inputs["L1_attn_norm_weight"].reshape(128, 1)),
        "woT": woT, "masklow": masklow, "maskup": maskup,
        "identC": np.eye(128, dtype=np.float32).astype(ml_dtypes.bfloat16),
        "identf": np.eye(128, dtype=np.float32),
        "crow": crow, "ccol": ccol,
    }


def run_C(inputs, x2):
    nc = build_C()
    in_maps = []
    for c in range(8):
        b, hh = c // 2, c % 2
        in_maps.append(prep_C_core(np.ascontiguousarray(x2[b].T), hh, inputs))
    res = run_bass_kernel_spmd(nc, in_maps, core_ids=list(range(8)))
    out = np.zeros((4, S, D), np.float32)
    for c in range(8):
        b = c // 2
        out[b] += res.results[c]["partial"].T
    return out



def kernel(**inputs):
    x = np.ascontiguousarray(np.asarray(inputs["L0_attn_x"], dtype=np.float32))

    ncA = build_A()
    in_maps = [prep_A_core(np.ascontiguousarray(x[c // 2].T), c % 2, inputs)
               for c in range(8)]
    resA = run_bass_kernel_spmd(ncA, in_maps, core_ids=list(range(8)))
    pA = [np.asarray(resA.results[c]["partial"]) for c in range(8)]

    ncB = build_B()
    wcache = {}
    in_maps = []
    for c in range(8):
        b, thf = c // 2, c % 2
        tsl = slice(thf * TOK, (thf + 1) * TOK)
        in_maps.append(prep_B_core(
            np.ascontiguousarray(x[b, tsl].T),
            np.ascontiguousarray(pA[2 * b][:, tsl]),
            np.ascontiguousarray(pA[2 * b + 1][:, tsl]),
            inputs, wcache))
    resB = run_bass_kernel_spmd(ncB, in_maps, core_ids=list(range(8)))
    x2T = [np.ascontiguousarray(np.concatenate(
        [np.asarray(resB.results[2 * b]["x2T"]),
         np.asarray(resB.results[2 * b + 1]["x2T"])], axis=1)) for b in range(4)]

    ncC = build_C()
    in_maps = [prep_C_core(x2T[c // 2], c % 2, inputs) for c in range(8)]
    resC = run_bass_kernel_spmd(ncC, in_maps, core_ids=list(range(8)))
    pC = [np.asarray(resC.results[c]["partial"]) for c in range(8)]

    ncD = build_D()
    wcache = {}
    in_maps = []
    for c in range(8):
        b, thf = c // 2, c % 2
        tsl = slice(thf * TOK, (thf + 1) * TOK)
        in_maps.append(prep_D_core(
            np.ascontiguousarray(x2T[b][:, tsl]),
            np.ascontiguousarray(pC[2 * b][:, tsl]),
            np.ascontiguousarray(pC[2 * b + 1][:, tsl]),
            inputs, wcache))
    resD = run_bass_kernel_spmd(ncD, in_maps, core_ids=list(range(8)))
    out = np.zeros((4, S, D), np.float32)
    for c in range(8):
        b, thf = c // 2, c % 2
        out[b, thf * TOK:(thf + 1) * TOK] = np.asarray(resD.results[c]["x4T"]).T
    return out
